# version 11
# baseline (speedup 1.0000x reference)
import numpy as np

import concourse.bass as bass
import concourse.mybir as mybir
from concourse.bass_utils import run_bass_kernel_spmd
from contextlib import ExitStack

F32 = mybir.dt.float32
BF16 = mybir.dt.bfloat16
AF = mybir.ActivationFunctionType
ALU = mybir.AluOpType
AX = mybir.AxisListType

NCORES = 8
BL = 2
S = 2048
E = 1024
H = 16
NP = 8
DHID = 4096
NKT = 16
QC = 512
NQC = 4
NTT = 16
LN_EPS = 1e-5
RW = 2432


class Buf:
    __slots__ = ("w", "rs")

    def __init__(self):
        self.w = None
        self.rs = []


import os as _os_
NQLANES = int(_os_.environ.get("NQLANES", "16"))


class IR:
    ENGS = ("t", "s", "v", "g") + tuple(f"q{k}" for k in range(NQLANES))

    def __init__(self):
        self.ops = []
        self.eng_ops = {e: [] for e in self.ENGS}
        self.bufs = {}
        self.qcount = 0
        self.qorder = []

    def _b(self, key):
        b = self.bufs.get(key)
        if b is None:
            b = self.bufs[key] = Buf()
        return b

    def op(self, eng, fn, r=(), w=(), accum=False, tag=""):
        i = len(self.ops)
        qlane_prev = None
        if eng == "q":
            lane = self.qcount % NQLANES
            self.qcount += 1
            eng = f"q{lane}"
            if self.eng_ops[eng]:
                qlane_prev = self.eng_ops[eng][-1]
            self.qorder.append(i)
        elif eng == "g":
            if self.eng_ops["g"]:
                qlane_prev = self.eng_ops["g"][-1]
        reads = [self._b(k) for k in r]
        writes = [self._b(k) for k in w]
        wkeys = tuple(w)
        deps = set()
        if qlane_prev is not None:
            deps.add(qlane_prev)
        for b in reads:
            if b.w is not None:
                deps.add(b.w)
        for b in writes:
            if b.w is not None:
                if not (accum and self.ops[b.w]["acc"] is not None
                        and set(self.ops[b.w]["acc"]) >= set(wkeys)):
                    deps.add(b.w)
            deps.update(b.rs)
        deps.discard(i)
        self.ops.append({"eng": eng, "fn": fn, "deps": deps, "tag": tag,
                         "acc": wkeys if accum else None,
                         "ord": len(self.eng_ops[eng])})
        self.eng_ops[eng].append(i)
        for b in reads:
            b.rs.append(i)
        for b in writes:
            b.w = i
            b.rs = []
        return i

    def finalize(self, nc, ctx):
        ops = self.ops
        n = len(ops)
        NEI = {e: k for k, e in enumerate(self.ENGS)}
        NE = len(self.ENGS)
        last = {e: None for e in self.ENGS}
        kept = [None] * n
        clocks = [None] * n
        qclock = [-1] * NE
        for i, o in enumerate(ops):
            e = o["eng"]
            is_q = e.startswith("q")
            if is_q:
                base = list(qclock)
            else:
                prev = last[e]
                base = list(clocks[prev]) if prev is not None else [-1] * NE
            kd = []
            for d in sorted(o["deps"], key=lambda d: -ops[d]["ord"]):
                de = ops[d]["eng"]
                if base[NEI[de]] >= ops[d]["ord"]:
                    continue
                kd.append(d)
                dc = clocks[d]
                for k in range(NE):
                    if dc[k] > base[k]:
                        base[k] = dc[k]
                if ops[d]["ord"] > base[NEI[de]]:
                    base[NEI[de]] = ops[d]["ord"]
            kept[i] = kd
            if is_q:
                qclock = base
                cc = list(base)
                if o["ord"] > cc[NEI[e]]:
                    cc[NEI[e]] = o["ord"]
                clocks[i] = cc
            else:
                clocks[i] = base
            last[e] = i
        need = [False] * n
        for i in range(n):
            for d in kept[i]:
                need[d] = True
        for i in self.qorder:
            need[i] = True
        for i in self.eng_ops["g"]:
            need[i] = True
        semval = [0] * n
        for e in self.ENGS:
            step = 16 if (e.startswith("q") or e == "g") else 1
            c = 0
            for i in self.eng_ops[e]:
                if need[i]:
                    c += step
                semval[i] = c
            assert c < 60000, (e, c)
        sems = {e: ctx.enter_context(nc.semaphore("sem_" + e))
                for e in self.ENGS}

        def emit_ops(engine, idxs):
            for i in idxs:
                o = ops[i]
                for d in kept[i]:
                    engine.wait_ge(sems[ops[d]["eng"]], semval[d])
                ins = o["fn"](engine)
                if need[i]:
                    e = o["eng"]
                    ins.then_inc(sems[e],
                                 16 if (e.startswith("q") or e == "g") else 1)

        def emit(e):
            return lambda engine: emit_ops(engine, self.eng_ops[e])

        def emit_q(engine):
            emit_ops(engine, self.qorder)
            for e in self.ENGS:
                if e.startswith("q") and self.eng_ops[e]:
                    engine.wait_ge(sems[e], semval[self.eng_ops[e][-1]])

        block = ctx.enter_context(nc.Block())
        block.tensor(emit("t"))
        block.scalar(emit("s"))
        block.vector(emit("v"))
        block.gpsimd(emit("g"))
        block.sync(emit_q)
        self.stats = {
            "n_ops": n,
            "n_waits": sum(len(k) for k in kept),
            "n_incs": sum(1 for x in need if x),
        }


GRAN = {"xtm": 1024, "qarr": 512, "karr": 512, "varr": 128, "aotm": 128,
        "trn": 128, "f32a": 128, "stat": 1, "bvec": 4, "cst": 64,
        "cstb": 33, "gbrow": 512}


def rg(name, lo, hi):
    g = GRAN[name]
    return tuple(f"{name}:{i}" for i in range(lo // g, (hi + g - 1) // g))


def pg(name, plo=0, phi=128):
    return tuple(f"{name}#{i}" for i in range(plo // 32, (phi + 31) // 32))


def _low_blocks(w, stage, transpose):
    s = 1 << stage
    out = np.zeros((32, 128, 128), np.float32)
    i = np.arange(DHID)
    ii = i[(i & s) == 0]
    jj = ii + s
    q = ii >> (stage + 1)
    t = ii & (s - 1)
    k = q * s + t
    w4 = np.asarray(w[stage], np.float32).reshape(-1, 2, 2)
    a, b, c, d = w4[k, 0, 0], w4[k, 0, 1], w4[k, 1, 0], w4[k, 1, 1]
    if transpose:
        b, c = c, b
    blk, li, lj = ii // 128, ii % 128, jj % 128
    out[blk, li, li] = a
    out[blk, li, lj] = b
    out[blk, lj, li] = c
    out[blk, lj, lj] = d
    return out


def _high_blocks(w, stage, transpose):
    st = 1 << stage
    sh = st >> 7
    out = np.zeros((32, 128, 128), np.float32)
    w4 = np.asarray(w[stage], np.float32).reshape(-1, 2, 2)
    hi = np.arange(32)
    h1 = hi[(hi & sh) == 0]
    h2 = h1 + sh
    for lo in range(128):
        tt, o = lo >> 2, 32 * (lo & 3)
        i = h1 * 128 + lo
        q = i >> (stage + 1)
        t = i & (st - 1)
        k = q * st + t
        a, b, c, d = w4[k, 0, 0], w4[k, 0, 1], w4[k, 1, 0], w4[k, 1, 1]
        if transpose:
            b, c = c, b
        out[tt, o + h1, o + h1] = a
        out[tt, o + h1, o + h2] = b
        out[tt, o + h2, o + h1] = c
        out[tt, o + h2, o + h2] = d
    return out


def _host_prep(inp):
    d = {}
    d["wqT"] = np.ascontiguousarray(np.asarray(inp["L0_attn_Wq"], np.float32).T)
    d["wkT"] = np.ascontiguousarray(np.asarray(inp["L0_attn_Wk"], np.float32).T)
    d["wvT"] = np.ascontiguousarray(np.asarray(inp["L0_attn_Wv"], np.float32).T)
    lg = np.asarray(inp["L0_attn_ln_gamma"], np.float32)
    lb = np.asarray(inp["L0_attn_ln_beta"], np.float32)
    d["gbc"] = np.ascontiguousarray(np.stack([
        np.broadcast_to(lg[None, :], (128, E)),
        np.broadcast_to(lb[None, :], (128, E))]))
    bt = np.asarray(inp["L0_attn_bias_table"], np.float32)
    m = np.arange(4096)
    teff = bt[np.minimum(m, 2046), :]
    p = np.arange(128)
    idx = m[None, :] - p[:, None]
    rall = teff[np.clip(idx, 0, 4095), :]
    rall = np.where((idx >= 0)[:, :, None], rall, 0.0)
    d["rall"] = np.ascontiguousarray(rall.transpose(2, 0, 1))
    wexp = np.asarray(inp["L0_ffn_W_exp"], np.float32)
    aug = np.zeros((DHID, 1040), np.float32)
    aug[:, 0:1024] = wexp.T
    aug[:, 1024] = np.asarray(inp["L0_ffn_b_exp"], np.float32)
    d["wexpT_aug"] = aug
    nidx = np.arange(DHID)
    oon = 128 * (nidx % 32) + nidx // 32
    d["wcon_lo"] = np.ascontiguousarray(
        np.asarray(inp["L0_ffn_W_con"], np.float32)[oon])
    d["upbias_lo"] = np.ascontiguousarray(
        np.asarray(inp["L0_ffn_up_bias"], np.float32)[oon].reshape(32, 128).T)
    d["bconf"] = np.ascontiguousarray(
        np.asarray(inp["L0_ffn_b_con"], np.float32).reshape(8, 128).T)
    d["dbiasf"] = np.ascontiguousarray(
        np.asarray(inp["L0_ffn_down_bias"], np.float32).reshape(8, 128).T)
    wu = np.asarray(inp["L0_ffn_up_weights"], np.float32)
    wd = np.asarray(inp["L0_ffn_down_weights"], np.float32)

    def _ff(st):
        n = st.shape[0]
        return np.ascontiguousarray(
            st.reshape(n, 4, 8, 128, 128).transpose(0, 1, 3, 2, 4)
            .reshape(n, 4, 128, 1024))

    d["upl"] = _ff(np.stack([_low_blocks(wu, 6, True)] +
                            [_low_blocks(wu, k, False)
                             for k in range(5, -1, -1)]))
    d["uph"] = _ff(np.stack([_high_blocks(wu, 11, True)] +
                            [_high_blocks(wu, 7 + j, False)
                             for j in range(3, -1, -1)]))
    d["dnh"] = _ff(np.stack([_high_blocks(wd, 7, False)] +
                            [_high_blocks(wd, 7 + j, True)
                             for j in range(1, 5)]))
    d["dnl"] = _ff(np.stack([_low_blocks(wd, 0, False)] +
                            [_low_blocks(wd, k, True) for k in range(1, 7)]))
    d["eye"] = np.eye(128, dtype=np.float32)
    d["ones"] = np.ones((128, 128), np.float32)
    return d


def _ap(h, off, dims):
    return bass.AP(h, off, [list(x) for x in dims])


def build_program(taps=(), race=False,
                  phases=("init", "fold", "pre", "ln1", "kproj", "vproj", "qproj", "att", "ffn")):
    import os
    _ph = set(os.environ.get("KPHASES", ",".join(phases)).split(","))
    nc = bass.Bass("TRN2", target_bir_lowering=False,
                   detect_race_conditions=race)
    ir = IR()

    din = {}
    for name, shape in (
        ("x", [BL, S, E]), ("wqT", [E, E]), ("wkT", [E, E]), ("wvT", [E, E]),
        ("gbc", [2, 128, E]), ("rall", [H, 128, 4096]),
        ("wexpT_aug", [DHID, 1040]), ("wcon_lo", [DHID, E]),
        ("upbias_lo", [128, 32]), ("bconf", [128, 8]), ("dbiasf", [128, 8]),
        ("upl", [7, 4, 128, 1024]), ("uph", [5, 4, 128, 1024]),
        ("dnh", [5, 4, 128, 1024]), ("dnl", [7, 4, 128, 1024]),
        ("eye", [128, 128]), ("ones", [128, 128]),
    ):
        din[name] = nc.dram_tensor(name, shape, F32, kind="ExternalInput")
    out_h = nc.dram_tensor("out", [BL, S, E], F32, kind="ExternalOutput")

    wq16 = nc.dram_tensor("wq16", [NP, 128, 1024], BF16, kind="Internal")
    wk16 = nc.dram_tensor("wk16", [NP, 128, 1024], BF16, kind="Internal")
    wv16 = nc.dram_tensor("wv16", [E, E], BF16, kind="Internal")
    r16 = nc.dram_tensor("r16", [H, 128, 4096], BF16, kind="Internal")
    zhb = nc.dram_tensor("zhb", [DHID, 1040], BF16, kind="Internal")
    v1hb = nc.dram_tensor("v1hb", [DHID, E], BF16, kind="Internal")
    weffh = nc.dram_tensor("weffh", [32, 128, 1024], BF16, kind="Internal")
    wconch = nc.dram_tensor("wconch", [8, 128, 4096], BF16, kind="Internal")
    q16 = nc.dram_tensor("q16", [NP, 128, S], BF16, kind="Internal")

    tap_h = {}
    for name, shape, tdt in taps:
        tap_h[name] = nc.dram_tensor("tap_" + name, list(shape), tdt,
                                     kind="ExternalOutput")

    ctx = ExitStack()
    with ctx:
        sb = lambda nm, sh, dt: ctx.enter_context(nc.sbuf_tensor(nm, sh, dt))
        pt = lambda nm, sh, dt=F32: ctx.enter_context(
            nc.psum_tensor(nm, sh, dt))

        xtm = sb("xtm", [128, NTT * E], F32)
        qarr = sb("qarr", [128, 16384], BF16)
        karr = sb("karr", [128, 16384], BF16)
        varr = sb("varr", [128, 16640], BF16)
        aotm = sb("aotm", [128, 4096], F32)
        trn = sb("trn", [128, 9984], BF16)
        trnf = trn.bitcast(F32)
        f32a = sb("f32a", [128, 2048], F32)
        stat = sb("stat", [128, 160], F32)
        cst = sb("cst", [128, 256], F32)
        cstb = sb("cstb", [128, 132], BF16)
        bvec = sb("bvec", [128, 96], F32)

        psA = [pt("psA0", [128, 512]), pt("psA1", [128, 512])]
        psB = [pt("psB0", [128, 512]), pt("psB1", [128, 512])]
        psO = pt("psO", [128, 512])
        psS = pt("psS", [128, 512])
        psC = pt("psC", [128, 512])
        psU = pt("psU", [128, 512], BF16)
        PSN = {id(psA[0]): "psA0", id(psA[1]): "psA1", id(psB[0]): "psB0",
               id(psB[1]): "psB1", id(psO): "psO", id(psS): "psS",
               id(psC): "psC", id(psU): "psU"}

        op = ir.op

        def dma(dst, src, r=(), w=(), tag="dma"):
            return op("q", lambda q, d=dst, s=src: q.dma_start(d, s),
                      r=r, w=w, tag=tag)

        def gdma(dst, src, r=(), w=(), tag="gdma"):
            return op("g", lambda g, d=dst, s=src: g.dma_start(d, s),
                      r=r, w=w, tag=tag)

        def tap(name, src_ap, rregs):
            if name in tap_h:
                t = tap_h[name]
                dst = t[:, :] if len(t.shape) == 2 else t[:, :, :]
                dma(dst, src_ap, r=rregs, w=("tap_" + name,), tag="tap")

        ident = cstb[:, 0:128]
        eyef = cst[:, 0:128]
        ones_col = cstb[:, 128:129]
        onesf_row = cst[0:1, 128:256]

        dma(cst[:, 0:128], din["eye"][:, :], w=rg("cst", 0, 128))
        dma(cst[:, 128:256], din["ones"][:, :], w=rg("cst", 128, 256))
        op("v", lambda v: v.tensor_copy(cstb[:, 0:128], cst[:, 0:128]),
           r=rg("cst", 0, 128), w=rg("cstb", 0, 128))
        op("v", lambda v: v.tensor_copy(cstb[:, 128:132], cst[:, 128:132]),
           r=rg("cst", 128, 132), w=rg("cstb", 128, 132))
        dma(bvec[:, 32:64], din["upbias_lo"][:, :], w=rg("bvec", 32, 64))
        dma(bvec[:, 64:72], din["bconf"][:, :], w=rg("bvec", 64, 72))
        dma(bvec[:, 72:80], din["dbiasf"][:, :], w=rg("bvec", 72, 80))
        op("v", lambda v: v.tensor_add(bvec[:, 80:88], bvec[:, 64:72],
                                       bvec[:, 72:80]),
           r=rg("bvec", 64, 80), w=rg("bvec", 80, 88))
        op("v", lambda v: v.memset(bvec[:, 88:89], LN_EPS),
           w=rg("bvec", 88, 89))

        cvt = [0]
        for wname, srch, dsth in (("wq16", din["wqT"], wq16),
                                  ("wk16", din["wkT"], wk16)):
            for ec in range(8):
                fs = 1024 * (cvt[0] % 2)
                cvt[0] += 1
                gdma(trn[:, fs:fs + 1024], srch[128 * ec:128 * (ec + 1), :],
                     w=rg("trn", fs, fs + 1024))
                dma(_ap(dsth, 128 * ec,
                        [(1024, 128), (131072, NP), (1, 128)]),
                    trn[:, fs:fs + 1024], r=rg("trn", fs, fs + 1024),
                    w=(wname,))
        for ec in range(8):
            fs = 1024 * (cvt[0] % 2)
            cvt[0] += 1
            gdma(trn[:, fs:fs + 1024], din["wvT"][128 * ec:128 * (ec + 1), :],
                 w=rg("trn", fs, fs + 1024))
            dma(wv16[128 * ec:128 * (ec + 1), :], trn[:, fs:fs + 1024],
                r=rg("trn", fs, fs + 1024), w=("wv16",))
        for h in range(H):
            for half in range(2):
                ts = 2048 + 2048 * (cvt[0] % 2)
                cvt[0] += 1
                gdma(trn[:, ts:ts + 2048],
                     din["rall"][h, :, 2048 * half:2048 * (half + 1)],
                     w=rg("trn", ts, ts + 2048))
                dma(r16[h, :, 2048 * half:2048 * (half + 1)],
                    trn[:, ts:ts + 2048],
                    r=rg("trn", ts, ts + 2048), w=("r16",))

        if "pre" in _ph:
            for i in range(NTT):
                dma(xtm[:, E * i:E * (i + 1)],
                    din["x"][0, 128 * i:128 * (i + 1), :],
                    r=("xin",), w=rg("xtm", E * i, E * (i + 1)), tag="xld")

        def fold_load_set(hsrc, hkey, i, sl):
            for g in range(4):
                src = _ap(hsrc, (i * 4 + g) * 128 * 1024,
                          [(1024, 128), (1, 1024)])
                gdma(qarr[:, 4096 * sl + 1024 * g:4096 * sl + 1024 * (g + 1)],
                     src, r=(hkey,),
                     w=rg("qarr", 4096 * sl + 1024 * g,
                          4096 * sl + 1024 * (g + 1)), tag="fold_ld")

        def compose_piece(hsrc, hkey, base, i):
            if i == 0:
                fold_load_set(hsrc, hkey, 0, 0)
                op("v", lambda v, base=base: v.tensor_copy(
                    varr[:, base:base + 4096], qarr[:, 0:4096]),
                   r=rg("qarr", 0, 4096), w=rg("varr", base, base + 4096))
                return
            sl = i % 4
            fold_load_set(hsrc, hkey, i, sl)
            for blk in range(32):
                pn = PSN[id(psA[blk % 2])]
                op("t", lambda t, base=base, blk=blk, sl=sl: t.matmul(
                    psA[blk % 2][:, 0:128],
                    qarr[:, 4096 * sl + 128 * blk:
                         4096 * sl + 128 * (blk + 1)],
                    varr[:, base + 128 * blk:base + 128 * (blk + 1)],
                    start=True, stop=True),
                    r=rg("qarr", 4096 * sl + 128 * blk,
                         4096 * sl + 128 * (blk + 1))
                    + rg("varr", base + 128 * blk, base + 128 * (blk + 1)),
                    w=pg(pn), tag="fmm")
                op("v", lambda v, base=base, blk=blk: v.tensor_copy(
                    varr[:, base + 128 * blk:base + 128 * (blk + 1)],
                    psA[blk % 2][:, 0:128]),
                    r=pg(pn),
                    w=rg("varr", base + 128 * blk, base + 128 * (blk + 1)),
                    tag="fev")

        MMCH = ((0, 512, 0), (512, 512, 1), (1024, 16, 2))
        PSETS = ((psA[0], psA[1], psB[0]), (psB[1], psO, psS))
        SRC0 = 4096

        def fold_apply(base, srcfn, srckey, nch, outfn, u):
            sl = u % 2
            so = SRC0 + 1152 * (u % 4)
            srcfn(u, so)
            for (o, wd, pi) in MMCH[:nch]:
                pb = PSETS[sl][pi]
                pn = PSN[id(pb)]
                op("t", lambda t, u=u, o=o, wd=wd, pb=pb, so=so: t.matmul(
                    pb[:, 0:wd], varr[:, base + 128 * u:base + 128 * (u + 1)],
                    trn[:, so + o:so + o + wd], start=True, stop=True),
                    r=rg("varr", base + 128 * u, base + 128 * (u + 1))
                    + rg("trn", so + o, so + o + wd),
                    w=pg(pn), tag="fmm")
            outfn(u)

        def ua1_src(blk, so):
            gdma(trn[:, so:so + 1040],
                 din["wexpT_aug"][128 * blk:128 * (blk + 1), :],
                 r=("wexpT",), w=rg("trn", so, so + 1040), tag="fold")

        def ua1_out(blk):
            oo = 2048 * (blk % 2)
            ps = PSETS[blk % 2]
            op("v", lambda v, oo=oo, ps=ps: v.tensor_copy(
                trn[:, oo:oo + 512], ps[0][:, 0:512]),
               r=pg(PSN[id(ps[0])]), w=rg("trn", oo, oo + 512))
            op("v", lambda v, oo=oo, ps=ps: v.tensor_copy(
                trn[:, oo + 512:oo + 1024], ps[1][:, 0:512]),
               r=pg(PSN[id(ps[1])]), w=rg("trn", oo + 512, oo + 1024))
            op("v", lambda v, oo=oo, ps=ps: v.tensor_copy(
                trn[:, oo + 1024:oo + 1040], ps[2][:, 0:16]),
               r=pg(PSN[id(ps[2])]), w=rg("trn", oo + 1024, oo + 1040))
            dma(_ap(zhb, 1040 * blk, [(32 * 1040, 128), (1, 1040)]),
                trn[:, oo:oo + 1040], r=rg("trn", oo, oo + 1040), w=("zhb",),
                tag="fold_sc")


        def ua2_src(t_, so):
            dma(trn[:, so:so + 1040], zhb[128 * t_:128 * (t_ + 1), :],
                r=("zhb",), w=rg("trn", so, so + 1040), tag="fold")

        def ua2_out(t_):
            oo = 1024 * (t_ % 2)
            to = 2048 + 1024 * (t_ % 2)
            ps = PSETS[t_ % 2]
            op("v", lambda v, oo=oo, ps=ps: v.tensor_copy(
                trn[:, oo:oo + 512], ps[0][:, 0:512]),
               r=pg(PSN[id(ps[0])]), w=rg("trn", oo, oo + 512))
            op("v", lambda v, oo=oo, ps=ps: v.tensor_copy(
                trn[:, oo + 512:oo + 1024], ps[1][:, 0:512]),
               r=pg(PSN[id(ps[1])]), w=rg("trn", oo + 512, oo + 1024))
            op("v", lambda v, t_=t_, ps=ps: v.tensor_copy(
                bvec[:, t_:t_ + 1], ps[2][:, 0:1]),
               r=pg(PSN[id(ps[2])]), w=rg("bvec", t_, t_ + 1))
            for g2 in range(2):
                for q4 in range(4):
                    ec = 4 * g2 + q4
                    op("t", lambda t, oo=oo, ec=ec, q4=q4: t.transpose(
                        psU[:, 128 * q4:128 * (q4 + 1)],
                        trn[:, oo + 128 * ec:oo + 128 * (ec + 1)], ident),
                        r=rg("trn", oo + 128 * ec, oo + 128 * (ec + 1))
                        + rg("cstb", 0, 128), w=("psUbank",), tag="ftr")
                op("v", lambda v, to=to, g2=g2: v.tensor_copy(
                    trn[:, to + 512 * g2:to + 512 * (g2 + 1)],
                    psU[:, 0:512]),
                    r=("psUbank",),
                    w=rg("trn", to + 512 * g2, to + 512 * (g2 + 1)))
            dma(weffh[t_, :, :], trn[:, to:to + 1024],
                r=rg("trn", to, to + 1024), w=("weffh",), tag="fold_st")


        def da1_src(t_, so):
            gdma(trn[:, so:so + 1024],
                 din["wcon_lo"][128 * t_:128 * (t_ + 1), :],
                 r=("wconlo",), w=rg("trn", so, so + 1024), tag="fold")

        def da1_out(t_):
            oo = 1024 * (t_ % 4)
            ps = PSETS[t_ % 2]
            op("v", lambda v, oo=oo, ps=ps: v.tensor_copy(
                trn[:, oo:oo + 512], ps[0][:, 0:512]),
               r=pg(PSN[id(ps[0])]), w=rg("trn", oo, oo + 512))
            op("v", lambda v, oo=oo, ps=ps: v.tensor_copy(
                trn[:, oo + 512:oo + 1024], ps[1][:, 0:512]),
               r=pg(PSN[id(ps[1])]), w=rg("trn", oo + 512, oo + 1024))
            dma(v1hb[128 * t_:128 * (t_ + 1), :], trn[:, oo:oo + 1024],
                r=rg("trn", oo, oo + 1024), w=("v1hb",), tag="fold_st")


        def da2_src(u, so):
            dma(trn[:, so:so + 1024],
                _ap(v1hb, 1024 * u, [(32 * 1024, 128), (1, 1024)]),
                r=("v1hb",), w=rg("trn", so, so + 1024), tag="fold")

        def da2_out(u):
            oo = 1024 * (u % 4)
            ps = PSETS[u % 2]
            op("v", lambda v, oo=oo, ps=ps: v.tensor_copy(
                trn[:, oo:oo + 512], ps[0][:, 0:512]),
               r=pg(PSN[id(ps[0])]), w=rg("trn", oo, oo + 512))
            op("v", lambda v, oo=oo, ps=ps: v.tensor_copy(
                trn[:, oo + 512:oo + 1024], ps[1][:, 0:512]),
               r=pg(PSN[id(ps[1])]), w=rg("trn", oo + 512, oo + 1024))
            for et in range(8):
                dst = _ap(wconch, et * 32 * 128 * 128 + u * 4096,
                          [(128, 32), (131072, 4), (1, 128)])
                dma(dst, trn[:, oo + 128 * et:oo + 128 * (et + 1)],
                    r=rg("trn", oo + 128 * et, oo + 128 * (et + 1)),
                    w=("wconch",), tag="fold_sc")

        if "fold" in _ph:
            for i in range(7):
                compose_piece(din["upl"], "upl", 0, i)
            pend = [(din["uph"], "uph", 4096, i) for i in range(5)]
            for blk in range(32):
                fold_apply(0, ua1_src, "wexpT", 3, ua1_out, blk)
                if blk % 6 == 3 and pend:
                    compose_piece(*pend.pop(0))
            while pend:
                compose_piece(*pend.pop(0))
            pend = [(din["dnh"], "dnh", 8192, i) for i in range(5)]
            for t_ in range(32):
                fold_apply(4096, ua2_src, "zhb", 3, ua2_out, t_)
                if t_ % 6 == 3 and pend:
                    compose_piece(*pend.pop(0))
            while pend:
                compose_piece(*pend.pop(0))
            op("v", lambda v: v.tensor_add(bvec[:, 0:32], bvec[:, 0:32],
                                           bvec[:, 32:64]),
               r=rg("bvec", 0, 64), w=rg("bvec", 0, 32))
            pend = [(din["dnl"], "dnl", 12288, i) for i in range(7)]
            for t_ in range(32):
                fold_apply(8192, da1_src, "wconlo", 2, da1_out, t_)
                if t_ % 4 == 2 and pend:
                    compose_piece(*pend.pop(0))
            while pend:
                compose_piece(*pend.pop(0))
            for u in range(32):
                fold_apply(12288, da2_src, "v1hb", 2, da2_out, u)

        def ln_stats(c0, c1, nt):
            op("v", lambda v: v.tensor_scalar(
                stat[:, 96:96 + nt], stat[:, c0 + 16:c0 + 16 + nt],
                1.0 / E, None, ALU.mult),
                r=rg("stat", c0 + 16, c0 + 16 + nt), w=rg("stat", 96, 96 + nt))
            op("v", lambda v: v.tensor_scalar(
                stat[:, 112:112 + nt], stat[:, c0:c0 + nt],
                1.0 / E, None, ALU.mult),
                r=rg("stat", c0, c0 + nt), w=rg("stat", 112, 112 + nt))
            op("v", lambda v: v.tensor_mul(
                stat[:, 128:128 + nt], stat[:, 96:96 + nt],
                stat[:, 96:96 + nt]),
                r=rg("stat", 96, 96 + nt), w=rg("stat", 128, 128 + nt))
            op("v", lambda v: v.tensor_sub(
                stat[:, 112:112 + nt], stat[:, 112:112 + nt],
                stat[:, 128:128 + nt]),
                r=rg("stat", 112, 112 + nt) + rg("stat", 128, 128 + nt),
                w=rg("stat", 112, 112 + nt))
            op("s", lambda s: s.activation(
                stat[:, 128:128 + nt], stat[:, 112:112 + nt], AF.Sqrt,
                bias=bvec[:, 88:89], scale=1.0),
                r=rg("stat", 112, 112 + nt) + rg("bvec", 88, 89),
                w=rg("stat", 128, 128 + nt))
            op("v", lambda v: v.reciprocal(
                stat[:, c1:c1 + nt], stat[:, 128:128 + nt]),
                r=rg("stat", 128, 128 + nt), w=rg("stat", c1, c1 + nt))
            op("v", lambda v: v.tensor_mul(
                stat[:, 144:144 + nt], stat[:, 96:96 + nt],
                stat[:, c1:c1 + nt]),
                r=rg("stat", 96, 96 + nt) + rg("stat", c1, c1 + nt),
                w=rg("stat", 144, 144 + nt))
            op("v", lambda v: v.tensor_scalar(
                stat[:, c1 + 16:c1 + 16 + nt], stat[:, 144:144 + nt],
                -1.0, None, ALU.mult),
                r=rg("stat", 144, 144 + nt),
                w=rg("stat", c1 + 16, c1 + 16 + nt))

        SQ = 8960
        for b in (range(BL) if "pre" in _ph else ()):
            for i in range(NTT):
                if b > 0:
                    dma(xtm[:, E * i:E * (i + 1)],
                        din["x"][b, 128 * i:128 * (i + 1), :],
                        r=("xin",), w=rg("xtm", E * i, E * (i + 1)),
                        tag="xld")

            for i in range(NTT):
                op("s", lambda s, i=i: s.activation(
                    trn[:, SQ:SQ + 1024], xtm[:, E * i:E * (i + 1)],
                    AF.Square, accum_out=stat[:, i:i + 1]),
                    r=rg("xtm", E * i, E * (i + 1)),
                    w=rg("trn", SQ, SQ + 1024) + rg("stat", i, i + 1),
                    tag="ln1")
                op("v", lambda v, i=i: v.reduce_sum(
                    stat[:, 16 + i:17 + i], xtm[:, E * i:E * (i + 1)],
                    axis=AX.X),
                    r=rg("xtm", E * i, E * (i + 1)),
                    w=rg("stat", 16 + i, 17 + i), tag="ln1")
            if "ln1" in _ph:
                ln_stats(0, 48, NTT)
            for i in (range(NTT) if "ln1" in _ph else ()):
                op("v", lambda v, i=i: v.tensor_scalar(
                    trn[:, SQ:SQ + 1024], xtm[:, E * i:E * (i + 1)],
                    stat[:, 48 + i:49 + i], stat[:, 64 + i:65 + i],
                    ALU.mult, ALU.add),
                    r=rg("xtm", E * i, E * (i + 1)) + rg("stat", 48 + i, 49 + i)
                    + rg("stat", 64 + i, 65 + i),
                    w=rg("trn", SQ, SQ + 1024), tag="ln1a")
                for g2 in range(2):
                    qw = ()
                    for q4 in range(4):
                        ec = 4 * g2 + q4
                        qw += rg("qarr", 2048 * ec + 128 * i,
                                 2048 * ec + 128 * (i + 1))
                        op("t", lambda t, ec=ec, q4=q4: t.transpose(
                            psU[:, 128 * q4:128 * (q4 + 1)],
                            trn[:, SQ + 128 * ec:SQ + 128 * (ec + 1)], ident),
                            r=rg("trn", SQ + 128 * ec, SQ + 128 * (ec + 1))
                            + rg("cstb", 0, 128), w=("psUbank",), tag="xhT")
                    op("v", lambda v, i=i, g2=g2: v.tensor_copy(
                        _ap(qarr, 2048 * 4 * g2 + 128 * i,
                            [(16384, 128), (2048, 4), (1, 128)]),
                        psU[:, 0:512].rearrange("p (c f) -> p c f", c=4)),
                        r=("psUbank",), w=qw, tag="xhTe")
            if b == 0:
                tap("xhfm", qarr[:, 0:1024], rg("qarr", 0, 1024))

            for j in (range(NP) if "kproj" in _ph else ()):
                ko = 1024 * (j % 2)
                dma(trn[:, ko:ko + 1024],
                    _ap(wk16, j * 128 * 1024,
                        [(1024, 128), (1, 1024)]),
                    r=("wk16",), w=rg("trn", ko, ko + 1024), tag="wkld")
                for kc in range(4):
                    pn = PSN[id(psA[kc % 2])]
                    for ec in range(8):
                        op("t", lambda t, kc=kc, ec=ec, ko=ko: t.matmul(
                            psA[kc % 2][:, 0:512],
                            trn[:, ko + 128 * ec:ko + 128 * (ec + 1)],
                            qarr[:, 2048 * ec + 512 * kc:
                                 2048 * ec + 512 * (kc + 1)],
                            start=(ec == 0), stop=(ec == 7)),
                            r=rg("trn", ko + 128 * ec, ko + 128 * (ec + 1))
                            + rg("qarr", 2048 * ec + 512 * kc,
                                 2048 * ec + 512 * (kc + 1)),
                            w=pg(pn), accum=(ec != 0), tag="kmm")
                    op("v", lambda v, j=j, kc=kc: v.tensor_copy(
                        karr[:, 2048 * j + 512 * kc:2048 * j + 512 * (kc + 1)],
                        psA[kc % 2][:, 0:512]),
                        r=pg(pn),
                        w=rg("karr", 2048 * j + 512 * kc,
                             2048 * j + 512 * (kc + 1)), tag="kev")
            if b == 0:
                tap("kall", karr[:, 0:1024], rg("karr", 0, 1024))

            for ec in (range(8) if "vproj" in _ph else ()):
                dma(trn[:, 1024 * ec:1024 * (ec + 1)],
                    wv16[128 * ec:128 * (ec + 1), :],
                    r=("wv16",),
                    w=rg("trn", 1024 * ec, 1024 * (ec + 1)),
                    tag="wvld")
            if "vproj" in _ph:
                op("v", lambda v: v.memset(
                    _ap(varr, 64, [(16640, 128), (65, 256), (1, 1)]), 1.0),
                    w=rg("varr", 0, 16640), tag="vones")
            for kt in (range(NKT) if "vproj" in _ph else ()):
                for fc in range(2):
                    pn = PSN[id(psB[fc])]
                    for ec in range(8):
                        op("t", lambda t, kt=kt, fc=fc, ec=ec: t.matmul(
                            psB[fc][:, 0:512],
                            qarr[:, 2048 * ec + 128 * kt:
                                 2048 * ec + 128 * (kt + 1)],
                            trn[:, 1024 * ec + 512 * fc:
                                 1024 * ec + 512 * (fc + 1)],
                            start=(ec == 0), stop=(ec == 7)),
                            r=rg("qarr", 2048 * ec + 128 * kt,
                                 2048 * ec + 128 * (kt + 1))
                            + rg("trn", 1024 * ec + 512 * fc,
                                 1024 * ec + 512 * (fc + 1)),
                            w=pg(pn), accum=(ec != 0), tag="vmm")
                    op("v", lambda v, kt=kt, fc=fc: v.tensor_copy(
                        _ap(varr, 1040 * kt + 520 * fc,
                            [(16640, 128), (65, 8), (1, 64)]),
                        psB[fc][:, 0:512].rearrange("p (h d) -> p h d", h=8)),
                        r=pg(pn),
                        w=rg("varr", 1040 * kt + 520 * fc,
                             1040 * kt + 520 * (fc + 1)), tag="vev")
            if b == 0:
                tap("vtm", _ap(varr, 0, [(16640, 128), (65, 16), (1, 64)]),
                    rg("varr", 0, 1040))

            for j in (range(NP) if "qproj" in _ph else ()):
                ko = 3072 * (j % 2)
                dma(trn[:, ko:ko + 1024],
                    _ap(wq16, j * 128 * 1024,
                        [(1024, 128), (1, 1024)]),
                    r=("wq16",), w=rg("trn", ko, ko + 1024), tag="wqld")
                qo = 1024 + 3584 * (j % 2)
                for kc in range(4):
                    pn = PSN[id(psA[kc % 2])]
                    for ec in range(8):
                        op("t", lambda t, kc=kc, ec=ec, ko=ko: t.matmul(
                            psA[kc % 2][:, 0:512],
                            trn[:, ko + 128 * ec:ko + 128 * (ec + 1)],
                            qarr[:, 2048 * ec + 512 * kc:
                                 2048 * ec + 512 * (kc + 1)],
                            start=(ec == 0), stop=(ec == 7)),
                            r=rg("trn", ko + 128 * ec, ko + 128 * (ec + 1))
                            + rg("qarr", 2048 * ec + 512 * kc,
                                 2048 * ec + 512 * (kc + 1)),
                            w=pg(pn), accum=(ec != 0), tag="qmm")
                    op("v", lambda v, kc=kc, qo=qo: v.tensor_copy(
                        trn[:, qo + 512 * kc:qo + 512 * (kc + 1)],
                        psA[kc % 2][:, 0:512]),
                        r=pg(pn),
                        w=rg("trn", qo + 512 * kc, qo + 512 * (kc + 1)),
                        tag="qev")
                dma(q16[j, :, :], trn[:, qo:qo + 2048],
                    r=rg("trn", qo, qo + 2048), w=(f"q16_{j}",), tag="qst")
            for qc in (range(NQC) if "att" in _ph else ()):
                qb = QC * qc
                qbase = 4096 * (qc % 2)
                for j in range(NP):
                    dma(qarr[:, qbase + 512 * j:qbase + 512 * (j + 1)],
                        q16[j, :, qb:qb + 512], r=(f"q16_{j}",),
                        w=rg("qarr", qbase + 512 * j, qbase + 512 * (j + 1)),
                        tag="qld")
                if b == 0 and qc == 0:
                    tap("qall", qarr[:, 0:1024], rg("qarr", 0, 1024))
                for j in range(NP):
                    rodd = j % 2
                    rbuf, rb0, rnm = ((qarr, 8192, "qarr") if rodd
                                      else (trn, 0, "trn"))
                    for hh in range(2):
                        dma(rbuf[:, rb0 + RW * hh:rb0 + RW * (hh + 1)],
                            r16[2 * j + hh, :, qb + 127:qb + 127 + RW],
                            r=("r16",),
                            w=rg(rnm, rb0 + RW * hh, rb0 + RW * (hh + 1)),
                            tag="rwin")
                    kreg = rg("karr", 2048 * j, 2048 * (j + 1))
                    qreg = rg("qarr", qbase + 512 * j, qbase + 512 * (j + 1))

                    def sc_ops(kt, j=j, qb=qb, qbase=qbase,
                               kreg=kreg, qreg=qreg):
                        par = kt % 2
                        pnA, pnB = PSN[id(psA[par])], PSN[id(psB[par])]
                        eoA = 4864 + 512 * (kt % 4)
                        eoB = 6912 + 512 * (kt % 4)
                        op("t", lambda t, j=j, kt=kt, par=par,
                           qbase=qbase: t.matmul(
                            psA[par][:, 0:512],
                            karr[0:64, 2048 * j + 128 * kt:
                                 2048 * j + 128 * (kt + 1)],
                            qarr[0:64, qbase + 512 * j:qbase + 512 * (j + 1)],
                            start=True, stop=True, tile_position=(0, 0)),
                            r=kreg + qreg, w=pg(pnA), tag="sc")
                        op("t", lambda t, j=j, kt=kt, par=par,
                           qbase=qbase: t.matmul(
                            psB[par][:, 0:512],
                            karr[64:128, 2048 * j + 128 * kt:
                                 2048 * j + 128 * (kt + 1)],
                            qarr[64:128, qbase + 512 * j:
                                 qbase + 512 * (j + 1)],
                            start=True, stop=True, tile_position=(64, 0)),
                            r=kreg + qreg, w=pg(pnB), tag="sc")
                        op("s", lambda s, eoA=eoA, par=par: s.activation(
                            trn[:, eoA:eoA + 512], psA[par][:, 0:512],
                            AF.Exp, scale=0.03125),
                            r=pg(pnA), w=rg("trn", eoA, eoA + 512), tag="exp")
                        op("s", lambda s, eoB=eoB, par=par: s.activation(
                            trn[:, eoB:eoB + 512], psB[par][:, 0:512],
                            AF.Exp, scale=0.03125),
                            r=pg(pnB), w=rg("trn", eoB, eoB + 512), tag="exp")

                    def avrb_ops(kt, j=j, rbuf=rbuf, rb0=rb0, rnm=rnm):
                        eoA = 4864 + 512 * (kt % 4)
                        eoB = 6912 + 512 * (kt % 4)
                        vb = 1040 * kt + 130 * j
                        vregA = rg("varr", vb, vb + 65)
                        vregB = rg("varr", vb + 65, vb + 130)
                        op("t", lambda t, kt=kt, vb=vb, eoA=eoA: t.matmul(
                            psO[0:65, :], varr[:, vb:vb + 65],
                            trn[:, eoA:eoA + 512],
                            start=(kt == 0), stop=(kt == NKT - 1),
                            tile_position=(0, 0)),
                            r=vregA + rg("trn", eoA, eoA + 512),
                            w=pg("psO", 0, 96), accum=(kt != 0), tag="av")
                        op("t", lambda t, kt=kt, vb=vb, eoB=eoB: t.matmul(
                            psS[0:65, :], varr[:, vb + 65:vb + 130],
                            trn[:, eoB:eoB + 512],
                            start=(kt == 0), stop=(kt == NKT - 1),
                            tile_position=(0, 0)),
                            r=vregB + rg("trn", eoB, eoB + 512),
                            w=pg("psS", 0, 96), accum=(kt != 0), tag="av")
                        roff = rb0 + 1920 - 128 * kt
                        op("t", lambda t, kt=kt, vb=vb, roff=roff,
                           rbuf=rbuf: t.matmul(
                            psC[0:64, :], varr[:, vb:vb + 64],
                            rbuf[:, roff:roff + 512],
                            start=(kt == 0), stop=(kt == NKT - 1),
                            tile_position=(0, 0)),
                            r=vregA + rg(rnm, roff, roff + 512),
                            w=pg("psC", 0, 64), accum=(kt != 0), tag="rb")
                        op("t", lambda t, kt=kt, vb=vb, roff=roff,
                           rbuf=rbuf: t.matmul(
                            psC[64:128, :], varr[:, vb + 65:vb + 129],
                            rbuf[:, RW + roff:RW + roff + 512],
                            start=(kt == 0), stop=(kt == NKT - 1),
                            tile_position=(0, 64), skip_group_check=True),
                            r=vregB + rg(rnm, RW + roff, RW + roff + 512),
                            w=pg("psC", 64, 128), accum=(kt != 0), tag="rb")

                    sc_ops(0)
                    sc_ops(1)
                    for kt in range(NKT):
                        avrb_ops(kt)
                        if kt + 2 < NKT:
                            sc_ops(kt + 2)

                    op("v", lambda v: v.reciprocal(f32a[64:65, 1024:1536],
                                                   psO[64:65, :]),
                       r=pg("psO", 64, 96), w=rg("f32a", 1024, 1536), tag="as")
                    op("v", lambda v: v.reciprocal(f32a[64:65, 1536:2048],
                                                   psS[64:65, :]),
                       r=pg("psS", 64, 96), w=rg("f32a", 1536, 2048), tag="as")
                    op("t", lambda t: t.matmul(
                        psA[0][0:64, :], cst[64:65, 128:192],
                        f32a[64:65, 1024:1536], start=True, stop=True,
                        tile_position=(64, 0)),
                        r=rg("f32a", 1024, 1536) + rg("cst", 128, 192),
                        w=pg("psA0", 0, 64), tag="as")
                    op("t", lambda t: t.matmul(
                        psA[0][64:128, :], cst[64:65, 128:192],
                        f32a[64:65, 1536:2048], start=True, stop=True,
                        tile_position=(64, 64), skip_group_check=True),
                        r=rg("f32a", 1536, 2048) + rg("cst", 128, 192),
                        w=pg("psA0", 64, 128), tag="as")
                    op("v", lambda v: v.tensor_copy(trnf[:, 4480:4992],
                                                    psA[0][:, 0:512]),
                       r=pg("psA0"), w=rg("trn", 8960, 9984), tag="as")
                    op("v", lambda v: v.tensor_tensor(
                        f32a[0:64, 0:512], psO[0:64, 0:512],
                        trnf[0:64, 4480:4992], ALU.mult),
                       r=pg("psO", 0, 64) + rg("trn", 8960, 9984),
                       w=rg("f32a", 0, 512), tag="as")
                    op("v", lambda v: v.tensor_tensor(
                        f32a[64:128, 0:512], psS[0:64, 0:512],
                        trnf[64:128, 4480:4992], ALU.mult),
                       r=pg("psS", 0, 64) + rg("trn", 8960, 9984),
                       w=rg("f32a", 0, 512), tag="as")
                    op("v", lambda v: v.tensor_tensor(
                        f32a[:, 512:1024], f32a[:, 0:512], psC[:, 0:512],
                        ALU.add),
                       r=rg("f32a", 0, 512) + pg("psC"),
                       w=rg("f32a", 512, 1024), tag="as")
                    for c in range(4):
                        op("t", lambda t, c=c: t.transpose(
                            psO[:, 128 * c:128 * (c + 1)],
                            f32a[:, 512 + 128 * c:512 + 128 * (c + 1)], eyef),
                            r=rg("f32a", 512 + 128 * c, 512 + 128 * (c + 1))
                            + rg("cst", 0, 128), w=pg("psO"), tag="asT")
                    aosegs = tuple(f"aotm:{(1024 * c + 128 * j) // 128}"
                                   for c in range(4))
                    op("v", lambda v, j=j: v.tensor_copy(
                        _ap(aotm, 128 * j, [(4096, 128), (1024, 4), (1, 128)]),
                        psO[:, 0:512].rearrange("p (c f) -> p c f", c=4)),
                        r=pg("psO"), w=aosegs, tag="asE")

                dma(f32a[:, 0:1024], din["gbc"][0, :, :], r=("gbc",),
                    w=rg("f32a", 0, 1024), tag="gbld")
                dma(trnf[:, 0:1024], din["gbc"][1, :, :], r=("gbc",),
                    w=rg("trn", 0, 2048), tag="gbld")
                for c in range(4):
                    aoreg = rg("aotm", 1024 * c, 1024 * (c + 1))
                    op("s", lambda s, c=c: s.activation(
                        trn[:, SQ:SQ + 1024], aotm[:, 1024 * c:1024 * (c + 1)],
                        AF.Square, accum_out=stat[:, c:c + 1]),
                        r=aoreg,
                        w=rg("trn", SQ, SQ + 1024) + rg("stat", c, c + 1),
                        tag="lnA")
                    op("v", lambda v, c=c: v.reduce_sum(
                        stat[:, 16 + c:17 + c],
                        aotm[:, 1024 * c:1024 * (c + 1)], axis=AX.X),
                        r=aoreg, w=rg("stat", 16 + c, 17 + c), tag="lnA")
                ln_stats(0, 48, 4)
                for c in range(4):
                    ti = 4 * qc + c
                    aoreg = rg("aotm", 1024 * c, 1024 * (c + 1))
                    op("v", lambda v, c=c: v.tensor_scalar(
                        f32a[:, 1024:2048], aotm[:, 1024 * c:1024 * (c + 1)],
                        stat[:, 48 + c:49 + c], stat[:, 64 + c:65 + c],
                        ALU.mult, ALU.add),
                        r=aoreg + rg("stat", 48 + c, 49 + c)
                        + rg("stat", 64 + c, 65 + c),
                        w=rg("f32a", 1024, 2048), tag="lnA2")
                    op("v", lambda v, c=c: v.tensor_mul(
                        aotm[:, 1024 * c:1024 * (c + 1)], f32a[:, 1024:2048],
                        f32a[:, 0:1024]),
                        r=rg("f32a", 1024, 2048) + rg("f32a", 0, 1024),
                        w=aoreg, tag="lnA2")
                    op("v", lambda v, c=c: v.tensor_add(
                        aotm[:, 1024 * c:1024 * (c + 1)],
                        aotm[:, 1024 * c:1024 * (c + 1)], trnf[:, 0:1024]),
                        r=aoreg + rg("trn", 0, 2048), w=aoreg, tag="lnA2")
                    op("s", lambda s, c=c: s.activation(
                        trn[:, SQ:SQ + 1024], aotm[:, 1024 * c:1024 * (c + 1)],
                        AF.Square, accum_out=stat[:, 32 + c:33 + c]),
                        r=aoreg,
                        w=rg("trn", SQ, SQ + 1024) + rg("stat", 32 + c, 33 + c),
                        tag="lnB")
                    op("v", lambda v, c=c: v.reduce_sum(
                        stat[:, 20 + c:21 + c],
                        aotm[:, 1024 * c:1024 * (c + 1)], axis=AX.X),
                        r=aoreg, w=rg("stat", 20 + c, 21 + c), tag="lnB")
                op("v", lambda v: v.tensor_scalar(
                    stat[:, 96:100], stat[:, 20:24], 1.0 / E, None, ALU.mult),
                    r=rg("stat", 20, 24), w=rg("stat", 96, 100))
                op("v", lambda v: v.tensor_scalar(
                    stat[:, 112:116], stat[:, 32:36], 1.0 / E, None, ALU.mult),
                    r=rg("stat", 32, 36), w=rg("stat", 112, 116))
                op("v", lambda v: v.tensor_mul(
                    stat[:, 128:132], stat[:, 96:100], stat[:, 96:100]),
                    r=rg("stat", 96, 100), w=rg("stat", 128, 132))
                op("v", lambda v: v.tensor_sub(
                    stat[:, 112:116], stat[:, 112:116], stat[:, 128:132]),
                    r=rg("stat", 112, 116) + rg("stat", 128, 132),
                    w=rg("stat", 112, 116))
                op("s", lambda s: s.activation(
                    stat[:, 128:132], stat[:, 112:116], AF.Sqrt,
                    bias=bvec[:, 88:89], scale=1.0),
                    r=rg("stat", 112, 116) + rg("bvec", 88, 89),
                    w=rg("stat", 128, 132))
                op("v", lambda v: v.reciprocal(stat[:, 80:84],
                                               stat[:, 128:132]),
                   r=rg("stat", 128, 132), w=rg("stat", 80, 84))
                op("v", lambda v: v.tensor_mul(
                    stat[:, 144:148], stat[:, 96:100], stat[:, 80:84]),
                    r=rg("stat", 96, 100) + rg("stat", 80, 84),
                    w=rg("stat", 144, 148))
                op("v", lambda v: v.tensor_scalar(
                    stat[:, 84:88], stat[:, 144:148], -1.0, None, ALU.mult),
                    r=rg("stat", 144, 148), w=rg("stat", 84, 88))
                for c in range(4):
                    ti = 4 * qc + c
                    aoreg = rg("aotm", 1024 * c, 1024 * (c + 1))
                    xreg = rg("xtm", E * ti, E * (ti + 1))
                    op("v", lambda v, c=c: v.tensor_scalar(
                        f32a[:, 1024:2048], aotm[:, 1024 * c:1024 * (c + 1)],
                        stat[:, 80 + c:81 + c], stat[:, 84 + c:85 + c],
                        ALU.mult, ALU.add),
                        r=aoreg + rg("stat", 80 + c, 81 + c)
                        + rg("stat", 84 + c, 85 + c),
                        w=rg("f32a", 1024, 2048), tag="lnB2")
                    op("v", lambda v, ti=ti: v.tensor_add(
                        xtm[:, E * ti:E * (ti + 1)],
                        xtm[:, E * ti:E * (ti + 1)], f32a[:, 1024:2048]),
                        r=xreg + rg("f32a", 1024, 2048), w=xreg, tag="resA")
                if b == 0 and qc == 0:
                    tap("x1", xtm[:, 0:1024], rg("xtm", 0, 1024))

            for tc in (range(NQC) if "ffn" in _ph else ()):
                for c in range(4):
                    ti = 4 * tc + c
                    xreg = rg("xtm", E * ti, E * (ti + 1))
                    op("s", lambda s, ti=ti, c=c: s.activation(
                        trn[:, SQ:SQ + 1024], xtm[:, E * ti:E * (ti + 1)],
                        AF.Square, accum_out=stat[:, c:c + 1]),
                        r=xreg,
                        w=rg("trn", SQ, SQ + 1024) + rg("stat", c, c + 1),
                        tag="ln3")
                    op("v", lambda v, ti=ti, c=c: v.reduce_sum(
                        stat[:, 16 + c:17 + c], xtm[:, E * ti:E * (ti + 1)],
                        axis=AX.X),
                        r=xreg, w=rg("stat", 16 + c, 17 + c), tag="ln3")
                ln_stats(0, 48, 4)
                for c in range(4):
                    ti = 4 * tc + c
                    xreg = rg("xtm", E * ti, E * (ti + 1))
                    op("v", lambda v, ti=ti, c=c: v.tensor_scalar(
                        trn[:, 4096:5120], xtm[:, E * ti:E * (ti + 1)],
                        stat[:, 48 + c:49 + c], stat[:, 64 + c:65 + c],
                        ALU.mult, ALU.add),
                        r=xreg + rg("stat", 48 + c, 49 + c)
                        + rg("stat", 64 + c, 65 + c),
                        w=rg("trn", 4096, 5120), tag="ln3a")
                    for g2 in range(2):
                        qw = ()
                        for q4 in range(4):
                            ec = 4 * g2 + q4
                            qw += rg("trn", 512 * ec + 128 * c,
                                     512 * ec + 128 * (c + 1))
                            op("t", lambda t, ec=ec, q4=q4: t.transpose(
                                psU[:, 128 * q4:128 * (q4 + 1)],
                                trn[:, 4096 + 128 * ec:4096 + 128 * (ec + 1)],
                                ident),
                                r=rg("trn", 4096 + 128 * ec,
                                     4096 + 128 * (ec + 1))
                                + rg("cstb", 0, 128), w=("psUbank",),
                                tag="x3T")
                        op("v", lambda v, c=c, g2=g2: v.tensor_copy(
                            _ap(trn, 512 * 4 * g2 + 128 * c,
                                [(9984, 128), (512, 4), (1, 128)]),
                            psU[:, 0:512].rearrange("p (q f) -> p q f", q=4)),
                            r=("psUbank",), w=qw, tag="x3Te")

                for ht in range(32):
                    wo = 5120 + 1024 * (ht % 2)
                    dma(trn[:, wo:wo + 1024],
                        _ap(weffh, ht * 128 * 1024,
                            [(1024, 128), (1, 1024)]),
                        r=("weffh",), w=rg("trn", wo, wo + 1024), tag="wefld")
                    pn = PSN[id(psA[ht % 2])]
                    for ec in range(8):
                        op("t", lambda t, ht=ht, ec=ec, wo=wo: t.matmul(
                            psA[ht % 2][:, 0:512],
                            trn[:, wo + 128 * ec:wo + 128 * (ec + 1)],
                            trn[:, 512 * ec:512 * (ec + 1)],
                            start=(ec == 0), stop=(ec == 7)),
                            r=rg("trn", wo + 128 * ec, wo + 128 * (ec + 1))
                            + rg("trn", 512 * ec, 512 * (ec + 1)),
                            w=pg(pn), accum=(ec != 0), tag="hmm")
                    op("s", lambda s, ht=ht: s.activation(
                        karr[:, 512 * ht:512 * (ht + 1)],
                        psA[ht % 2][:, 0:512], AF.Gelu,
                        bias=bvec[:, ht:ht + 1], scale=1.0),
                        r=pg(pn) + rg("bvec", ht, ht + 1),
                        w=rg("karr", 512 * ht, 512 * (ht + 1)), tag="gelu")
                if b == 0 and tc == 0:
                    tap("hid", karr[:, 0:1024], rg("karr", 0, 1024))

                for et in range(8):
                    wo = 4096 * (et % 2)
                    dma(trn[:, wo:wo + 4096],
                        _ap(wconch, et * 32 * 128 * 128,
                            [(4096, 128), (1, 4096)]),
                        r=("wconch",), w=rg("trn", wo, wo + 4096), tag="wccld")
                    pn = PSN[id(psB[et % 2])]
                    for ht in range(32):
                        op("t", lambda t, et=et, ht=ht, wo=wo: t.matmul(
                            psB[et % 2][:, 0:512],
                            trn[:, wo + 128 * ht:wo + 128 * (ht + 1)],
                            karr[:, 512 * ht:512 * (ht + 1)],
                            start=(ht == 0), stop=(ht == 31)),
                            r=rg("trn", wo + 128 * ht, wo + 128 * (ht + 1))
                            + rg("karr", 512 * ht, 512 * (ht + 1)),
                            w=pg(pn), accum=(ht != 0), tag="ymm")
                    op("v", lambda v, et=et: v.tensor_scalar(
                        aotm[:, 512 * et:512 * (et + 1)],
                        psB[et % 2][:, 0:512], bvec[:, 80 + et:81 + et],
                        None, ALU.add),
                        r=pg(pn) + rg("bvec", 80 + et, 81 + et),
                        w=rg("aotm", 512 * et, 512 * (et + 1)), tag="yev")
                if b == 0 and tc == 0:
                    tap("yfm", aotm[:, 0:1024], rg("aotm", 0, 1024))

                for c in range(4):
                    ti = 4 * tc + c
                    for g2 in range(2):
                        for q4 in range(4):
                            et = 4 * g2 + q4
                            op("t", lambda t, et=et, c=c, q4=q4: t.transpose(
                                psO[:, 128 * q4:128 * (q4 + 1)],
                                aotm[:, 512 * et + 128 * c:
                                     512 * et + 128 * (c + 1)], eyef),
                                r=rg("aotm", 512 * et + 128 * c,
                                     512 * et + 128 * (c + 1))
                                + rg("cst", 0, 128),
                                w=pg("psO"), tag="yT")
                        op("v", lambda v, g2=g2: v.tensor_copy(
                            f32a[:, 512 * g2:512 * (g2 + 1)],
                            psO[:, 0:512]),
                            r=pg("psO"),
                            w=rg("f32a", 512 * g2, 512 * (g2 + 1)), tag="yTe")
                    op("s", lambda s: s.activation(
                        trn[:, SQ:SQ + 1024], f32a[:, 0:1024], AF.Square,
                        accum_out=stat[:, 0:1]),
                        r=rg("f32a", 0, 1024),
                        w=rg("trn", SQ, SQ + 1024) + rg("stat", 0, 1),
                        tag="ln4")
                    op("v", lambda v: v.reduce_sum(
                        stat[:, 16:17], f32a[:, 0:1024], axis=AX.X),
                        r=rg("f32a", 0, 1024), w=rg("stat", 16, 17), tag="ln4")
                    ln_stats(0, 48, 1)
                    op("v", lambda v: v.tensor_scalar(
                        f32a[:, 1024:2048], f32a[:, 0:1024],
                        stat[:, 48:49], stat[:, 64:65], ALU.mult, ALU.add),
                        r=rg("f32a", 0, 1024) + rg("stat", 48, 49)
                        + rg("stat", 64, 65),
                        w=rg("f32a", 1024, 2048), tag="ln4a")
                    xreg = rg("xtm", E * ti, E * (ti + 1))
                    op("v", lambda v, ti=ti: v.tensor_add(
                        xtm[:, E * ti:E * (ti + 1)],
                        xtm[:, E * ti:E * (ti + 1)], f32a[:, 1024:2048]),
                        r=xreg + rg("f32a", 1024, 2048), w=xreg, tag="resF")
                    dma(out_h[b, 128 * ti:128 * (ti + 1), :],
                        xtm[:, E * ti:E * (ti + 1)], r=xreg,
                        w=(f"out{b}_{ti}",), tag="outst")

        if "ffn" not in _ph or "pre" not in _ph:
            for ti in range(2):
                dma(xtm[:, E * ti:E * (ti + 1)],
                    din["x"][0, 128 * ti:128 * (ti + 1), :],
                    r=("xin",), w=rg("xtm", E * ti, E * (ti + 1)))
                dma(out_h[0, 128 * ti:128 * (ti + 1), :],
                    xtm[:, E * ti:E * (ti + 1)],
                    r=rg("xtm", E * ti, E * (ti + 1)),
                    w=(f"outz{ti}",), tag="outst")
        ir.finalize(nc, ctx)
    return nc, tap_h, ir


_CACHED = {}


def kernel(**inputs):
    x = np.asarray(inputs["L0_attn_x"], np.float32)
    prep = _host_prep(inputs)
    if "nc" not in _CACHED:
        _CACHED["nc"] = build_program()[0]
    nc = _CACHED["nc"]
    in_maps = []
    for c in range(NCORES):
        m = {"x": np.ascontiguousarray(x[BL * c:BL * (c + 1)])}
        m.update(prep)
        in_maps.append(m)
    res = run_bass_kernel_spmd(nc, in_maps, core_ids=list(range(NCORES)))
    return np.concatenate([r["out"] for r in res.results], axis=0)


# version 12
# speedup vs baseline: 1.0153x; 1.0153x over previous
import numpy as np

import concourse.bass as bass
import concourse.mybir as mybir
from concourse.bass_utils import run_bass_kernel_spmd
from contextlib import ExitStack

F32 = mybir.dt.float32
BF16 = mybir.dt.bfloat16
AF = mybir.ActivationFunctionType
ALU = mybir.AluOpType
AX = mybir.AxisListType

NCORES = 8
BL = 2
S = 2048
E = 1024
H = 16
NP = 8
DHID = 4096
NKT = 16
QC = 512
NQC = 4
NTT = 16
LN_EPS = 1e-5
RW = 2432


class Buf:
    __slots__ = ("w", "rs")

    def __init__(self):
        self.w = None
        self.rs = []


import os as _os_
NQLANES = int(_os_.environ.get("NQLANES", "16"))


class IR:
    ENGS = ("t", "s", "v", "g") + tuple(f"q{k}" for k in range(NQLANES))

    def __init__(self):
        self.ops = []
        self.eng_ops = {e: [] for e in self.ENGS}
        self.bufs = {}
        self.qcount = 0
        self.qorder = []

    def _b(self, key):
        b = self.bufs.get(key)
        if b is None:
            b = self.bufs[key] = Buf()
        return b

    def op(self, eng, fn, r=(), w=(), accum=False, tag=""):
        i = len(self.ops)
        qlane_prev = None
        if eng == "q":
            lane = self.qcount % NQLANES
            self.qcount += 1
            eng = f"q{lane}"
            if self.eng_ops[eng]:
                qlane_prev = self.eng_ops[eng][-1]
            self.qorder.append(i)
        elif eng == "g":
            if self.eng_ops["g"]:
                qlane_prev = self.eng_ops["g"][-1]
        reads = [self._b(k) for k in r]
        writes = [self._b(k) for k in w]
        wkeys = tuple(w)
        deps = set()
        if qlane_prev is not None:
            deps.add(qlane_prev)
        for b in reads:
            if b.w is not None:
                deps.add(b.w)
        for b in writes:
            if b.w is not None:
                if not (accum and self.ops[b.w]["acc"] is not None
                        and set(self.ops[b.w]["acc"]) >= set(wkeys)):
                    deps.add(b.w)
            deps.update(b.rs)
        deps.discard(i)
        self.ops.append({"eng": eng, "fn": fn, "deps": deps, "tag": tag,
                         "acc": wkeys if accum else None,
                         "ord": len(self.eng_ops[eng])})
        self.eng_ops[eng].append(i)
        for b in reads:
            b.rs.append(i)
        for b in writes:
            b.w = i
            b.rs = []
        return i

    def finalize(self, nc, ctx):
        ops = self.ops
        n = len(ops)
        NEI = {e: k for k, e in enumerate(self.ENGS)}
        NE = len(self.ENGS)
        last = {e: None for e in self.ENGS}
        kept = [None] * n
        clocks = [None] * n
        qclock = [-1] * NE
        for i, o in enumerate(ops):
            e = o["eng"]
            is_q = e.startswith("q")
            if is_q:
                base = list(qclock)
            else:
                prev = last[e]
                base = list(clocks[prev]) if prev is not None else [-1] * NE
            kd = []
            for d in sorted(o["deps"], key=lambda d: -ops[d]["ord"]):
                de = ops[d]["eng"]
                if base[NEI[de]] >= ops[d]["ord"]:
                    continue
                kd.append(d)
                dc = clocks[d]
                for k in range(NE):
                    if dc[k] > base[k]:
                        base[k] = dc[k]
                if ops[d]["ord"] > base[NEI[de]]:
                    base[NEI[de]] = ops[d]["ord"]
            kept[i] = kd
            if is_q:
                qclock = base
                cc = list(base)
                if o["ord"] > cc[NEI[e]]:
                    cc[NEI[e]] = o["ord"]
                clocks[i] = cc
            else:
                clocks[i] = base
            last[e] = i
        need = [False] * n
        for i in range(n):
            for d in kept[i]:
                need[d] = True
        for i in self.qorder:
            need[i] = True
        for i in self.eng_ops["g"]:
            need[i] = True
        semval = [0] * n
        for e in self.ENGS:
            step = 16 if (e.startswith("q") or e == "g") else 1
            c = 0
            for i in self.eng_ops[e]:
                if need[i]:
                    c += step
                semval[i] = c
            assert c < 60000, (e, c)
        sems = {e: ctx.enter_context(nc.semaphore("sem_" + e))
                for e in self.ENGS}

        def emit_ops(engine, idxs):
            for i in idxs:
                o = ops[i]
                for d in kept[i]:
                    engine.wait_ge(sems[ops[d]["eng"]], semval[d])
                ins = o["fn"](engine)
                if need[i]:
                    e = o["eng"]
                    ins.then_inc(sems[e],
                                 16 if (e.startswith("q") or e == "g") else 1)

        def emit(e):
            return lambda engine: emit_ops(engine, self.eng_ops[e])

        def emit_q(engine):
            emit_ops(engine, self.qorder)
            for e in self.ENGS:
                if e.startswith("q") and self.eng_ops[e]:
                    engine.wait_ge(sems[e], semval[self.eng_ops[e][-1]])

        block = ctx.enter_context(nc.Block())
        block.tensor(emit("t"))
        block.scalar(emit("s"))
        block.vector(emit("v"))
        block.gpsimd(emit("g"))
        block.sync(emit_q)
        self.stats = {
            "n_ops": n,
            "n_waits": sum(len(k) for k in kept),
            "n_incs": sum(1 for x in need if x),
        }


GRAN = {"xtm": 1024, "qarr": 512, "karr": 512, "varr": 128, "aotm": 128,
        "trn": 128, "f32a": 128, "stat": 1, "bvec": 4, "cst": 64,
        "cstb": 33, "gbrow": 512}


def rg(name, lo, hi):
    g = GRAN[name]
    return tuple(f"{name}:{i}" for i in range(lo // g, (hi + g - 1) // g))


def pg(name, plo=0, phi=128):
    return tuple(f"{name}#{i}" for i in range(plo // 32, (phi + 31) // 32))


def _low_blocks(w, stage, transpose):
    s = 1 << stage
    out = np.zeros((32, 128, 128), np.float32)
    i = np.arange(DHID)
    ii = i[(i & s) == 0]
    jj = ii + s
    q = ii >> (stage + 1)
    t = ii & (s - 1)
    k = q * s + t
    w4 = np.asarray(w[stage], np.float32).reshape(-1, 2, 2)
    a, b, c, d = w4[k, 0, 0], w4[k, 0, 1], w4[k, 1, 0], w4[k, 1, 1]
    if transpose:
        b, c = c, b
    blk, li, lj = ii // 128, ii % 128, jj % 128
    out[blk, li, li] = a
    out[blk, li, lj] = b
    out[blk, lj, li] = c
    out[blk, lj, lj] = d
    return out


def _high_blocks(w, stage, transpose):
    st = 1 << stage
    sh = st >> 7
    out = np.zeros((32, 128, 128), np.float32)
    w4 = np.asarray(w[stage], np.float32).reshape(-1, 2, 2)
    hi = np.arange(32)
    h1 = hi[(hi & sh) == 0]
    h2 = h1 + sh
    for lo in range(128):
        tt, o = lo >> 2, 32 * (lo & 3)
        i = h1 * 128 + lo
        q = i >> (stage + 1)
        t = i & (st - 1)
        k = q * st + t
        a, b, c, d = w4[k, 0, 0], w4[k, 0, 1], w4[k, 1, 0], w4[k, 1, 1]
        if transpose:
            b, c = c, b
        out[tt, o + h1, o + h1] = a
        out[tt, o + h1, o + h2] = b
        out[tt, o + h2, o + h1] = c
        out[tt, o + h2, o + h2] = d
    return out


def _host_prep(inp):
    d = {}
    d["wqT"] = np.ascontiguousarray(np.asarray(inp["L0_attn_Wq"], np.float32).T)
    d["wkT"] = np.ascontiguousarray(np.asarray(inp["L0_attn_Wk"], np.float32).T)
    d["wvT"] = np.ascontiguousarray(np.asarray(inp["L0_attn_Wv"], np.float32).T)
    lg = np.asarray(inp["L0_attn_ln_gamma"], np.float32)
    lb = np.asarray(inp["L0_attn_ln_beta"], np.float32)
    d["gbc"] = np.ascontiguousarray(np.stack([
        np.broadcast_to(lg[None, :], (128, E)),
        np.broadcast_to(lb[None, :], (128, E))]))
    bt = np.asarray(inp["L0_attn_bias_table"], np.float32)
    m = np.arange(4096)
    teff = bt[np.minimum(m, 2046), :]
    p = np.arange(128)
    idx = m[None, :] - p[:, None]
    rall = teff[np.clip(idx, 0, 4095), :]
    rall = np.where((idx >= 0)[:, :, None], rall, 0.0)
    d["rall"] = np.ascontiguousarray(rall.transpose(2, 0, 1))
    wexp = np.asarray(inp["L0_ffn_W_exp"], np.float32)
    aug = np.zeros((DHID, 1040), np.float32)
    aug[:, 0:1024] = wexp.T
    aug[:, 1024] = np.asarray(inp["L0_ffn_b_exp"], np.float32)
    d["wexpT_aug"] = aug
    nidx = np.arange(DHID)
    oon = 128 * (nidx % 32) + nidx // 32
    d["wcon_lo"] = np.ascontiguousarray(
        np.asarray(inp["L0_ffn_W_con"], np.float32)[oon])
    d["upbias_lo"] = np.ascontiguousarray(
        np.asarray(inp["L0_ffn_up_bias"], np.float32)[oon].reshape(32, 128).T)
    d["bconf"] = np.ascontiguousarray(
        np.asarray(inp["L0_ffn_b_con"], np.float32).reshape(8, 128).T)
    d["dbiasf"] = np.ascontiguousarray(
        np.asarray(inp["L0_ffn_down_bias"], np.float32).reshape(8, 128).T)
    wu = np.asarray(inp["L0_ffn_up_weights"], np.float32)
    wd = np.asarray(inp["L0_ffn_down_weights"], np.float32)

    def _ff(st):
        n = st.shape[0]
        return np.ascontiguousarray(
            st.reshape(n, 4, 8, 128, 128).transpose(0, 1, 3, 2, 4)
            .reshape(n, 4, 128, 1024))

    d["upl"] = _ff(np.stack([_low_blocks(wu, 6, True)] +
                            [_low_blocks(wu, k, False)
                             for k in range(5, -1, -1)]))
    d["uph"] = _ff(np.stack([_high_blocks(wu, 11, True)] +
                            [_high_blocks(wu, 7 + j, False)
                             for j in range(3, -1, -1)]))
    d["dnh"] = _ff(np.stack([_high_blocks(wd, 7, False)] +
                            [_high_blocks(wd, 7 + j, True)
                             for j in range(1, 5)]))
    d["dnl"] = _ff(np.stack([_low_blocks(wd, 0, False)] +
                            [_low_blocks(wd, k, True) for k in range(1, 7)]))
    d["eye"] = np.eye(128, dtype=np.float32)
    d["ones"] = np.ones((128, 128), np.float32)
    return d


def _ap(h, off, dims):
    return bass.AP(h, off, [list(x) for x in dims])


def build_program(taps=(), race=False,
                  phases=("init", "fold", "pre", "ln1", "kproj", "vproj", "qproj", "att", "ffn")):
    import os
    _ph = set(os.environ.get("KPHASES", ",".join(phases)).split(","))
    nc = bass.Bass("TRN2", target_bir_lowering=False,
                   detect_race_conditions=race)
    ir = IR()

    din = {}
    for name, shape in (
        ("x", [BL, S, E]), ("wqT", [E, E]), ("wkT", [E, E]), ("wvT", [E, E]),
        ("gbc", [2, 128, E]), ("rall", [H, 128, 4096]),
        ("wexpT_aug", [DHID, 1040]), ("wcon_lo", [DHID, E]),
        ("upbias_lo", [128, 32]), ("bconf", [128, 8]), ("dbiasf", [128, 8]),
        ("upl", [7, 4, 128, 1024]), ("uph", [5, 4, 128, 1024]),
        ("dnh", [5, 4, 128, 1024]), ("dnl", [7, 4, 128, 1024]),
        ("eye", [128, 128]), ("ones", [128, 128]),
    ):
        din[name] = nc.dram_tensor(name, shape, F32, kind="ExternalInput")
    out_h = nc.dram_tensor("out", [BL, S, E], F32, kind="ExternalOutput")

    wq16 = nc.dram_tensor("wq16", [NP, 128, 1024], BF16, kind="Internal")
    wk16 = nc.dram_tensor("wk16", [NP, 128, 1024], BF16, kind="Internal")
    wv16 = nc.dram_tensor("wv16", [E, E], BF16, kind="Internal")
    r16 = nc.dram_tensor("r16", [H, 128, 4096], BF16, kind="Internal")
    zhb = nc.dram_tensor("zhb", [DHID, 1040], BF16, kind="Internal")
    v1hb = nc.dram_tensor("v1hb", [DHID, E], BF16, kind="Internal")
    weffh = nc.dram_tensor("weffh", [32, 128, 1024], BF16, kind="Internal")
    wconch = nc.dram_tensor("wconch", [8, 128, 4096], BF16, kind="Internal")
    q16 = nc.dram_tensor("q16", [NP, 128, S], BF16, kind="Internal")

    tap_h = {}
    for name, shape, tdt in taps:
        tap_h[name] = nc.dram_tensor("tap_" + name, list(shape), tdt,
                                     kind="ExternalOutput")

    ctx = ExitStack()
    with ctx:
        sb = lambda nm, sh, dt: ctx.enter_context(nc.sbuf_tensor(nm, sh, dt))
        pt = lambda nm, sh, dt=F32: ctx.enter_context(
            nc.psum_tensor(nm, sh, dt))

        xtm = sb("xtm", [128, NTT * E], F32)
        qarr = sb("qarr", [128, 16384], BF16)
        karr = sb("karr", [128, 16384], BF16)
        varr = sb("varr", [128, 16640], BF16)
        aotm = sb("aotm", [128, 4096], F32)
        trn = sb("trn", [128, 9984], BF16)
        trnf = trn.bitcast(F32)
        kf = karr.bitcast(F32)
        f32a = sb("f32a", [128, 2048], F32)
        stat = sb("stat", [128, 160], F32)
        cst = sb("cst", [128, 256], F32)
        cstb = sb("cstb", [128, 132], BF16)
        bvec = sb("bvec", [128, 96], F32)

        psA = [pt("psA0", [128, 512]), pt("psA1", [128, 512])]
        psB = [pt("psB0", [128, 512]), pt("psB1", [128, 512])]
        psO = pt("psO", [128, 512])
        psS = pt("psS", [128, 512])
        psC = pt("psC", [128, 512])
        psU = pt("psU", [128, 512], BF16)
        PSN = {id(psA[0]): "psA0", id(psA[1]): "psA1", id(psB[0]): "psB0",
               id(psB[1]): "psB1", id(psO): "psO", id(psS): "psS",
               id(psC): "psC", id(psU): "psU"}

        op = ir.op

        def dma(dst, src, r=(), w=(), tag="dma"):
            return op("q", lambda q, d=dst, s=src: q.dma_start(d, s),
                      r=r, w=w, tag=tag)

        def gdma(dst, src, r=(), w=(), tag="gdma"):
            return op("g", lambda g, d=dst, s=src: g.dma_start(d, s),
                      r=r, w=w, tag=tag)

        def tap(name, src_ap, rregs):
            if name in tap_h:
                t = tap_h[name]
                dst = t[:, :] if len(t.shape) == 2 else t[:, :, :]
                dma(dst, src_ap, r=rregs, w=("tap_" + name,), tag="tap")

        ident = cstb[:, 0:128]
        eyef = cst[:, 0:128]
        ones_col = cstb[:, 128:129]
        onesf_row = cst[0:1, 128:256]

        dma(cst[:, 0:128], din["eye"][:, :], w=rg("cst", 0, 128))
        dma(cst[:, 128:256], din["ones"][:, :], w=rg("cst", 128, 256))
        op("v", lambda v: v.tensor_copy(cstb[:, 0:128], cst[:, 0:128]),
           r=rg("cst", 0, 128), w=rg("cstb", 0, 128))
        op("v", lambda v: v.tensor_copy(cstb[:, 128:132], cst[:, 128:132]),
           r=rg("cst", 128, 132), w=rg("cstb", 128, 132))
        dma(bvec[:, 32:64], din["upbias_lo"][:, :], w=rg("bvec", 32, 64))
        dma(bvec[:, 64:72], din["bconf"][:, :], w=rg("bvec", 64, 72))
        dma(bvec[:, 72:80], din["dbiasf"][:, :], w=rg("bvec", 72, 80))
        op("v", lambda v: v.tensor_add(bvec[:, 80:88], bvec[:, 64:72],
                                       bvec[:, 72:80]),
           r=rg("bvec", 64, 80), w=rg("bvec", 80, 88))
        op("v", lambda v: v.memset(bvec[:, 88:89], LN_EPS),
           w=rg("bvec", 88, 89))

        cvt = [0]
        for wname, srch, dsth in (("wq16", din["wqT"], wq16),
                                  ("wk16", din["wkT"], wk16)):
            for ec in range(8):
                fs = 1024 * (cvt[0] % 2)
                cvt[0] += 1
                gdma(trn[:, fs:fs + 1024], srch[128 * ec:128 * (ec + 1), :],
                     w=rg("trn", fs, fs + 1024))
                dma(_ap(dsth, 128 * ec,
                        [(1024, 128), (131072, NP), (1, 128)]),
                    trn[:, fs:fs + 1024], r=rg("trn", fs, fs + 1024),
                    w=(wname,))
        for ec in range(8):
            fs = 1024 * (cvt[0] % 2)
            cvt[0] += 1
            gdma(trn[:, fs:fs + 1024], din["wvT"][128 * ec:128 * (ec + 1), :],
                 w=rg("trn", fs, fs + 1024))
            dma(wv16[128 * ec:128 * (ec + 1), :], trn[:, fs:fs + 1024],
                r=rg("trn", fs, fs + 1024), w=("wv16",))
        for h in range(H):
            for half in range(2):
                ts = 2048 + 2048 * (cvt[0] % 2)
                cvt[0] += 1
                gdma(trn[:, ts:ts + 2048],
                     din["rall"][h, :, 2048 * half:2048 * (half + 1)],
                     w=rg("trn", ts, ts + 2048))
                dma(r16[h, :, 2048 * half:2048 * (half + 1)],
                    trn[:, ts:ts + 2048],
                    r=rg("trn", ts, ts + 2048), w=("r16",))

        if "pre" in _ph:
            for i in range(NTT):
                dma(xtm[:, E * i:E * (i + 1)],
                    din["x"][0, 128 * i:128 * (i + 1), :],
                    r=("xin",), w=rg("xtm", E * i, E * (i + 1)), tag="xld")

        def fold_load_set(hsrc, hkey, i, sl):
            for g in range(4):
                src = _ap(hsrc, (i * 4 + g) * 128 * 1024,
                          [(1024, 128), (1, 1024)])
                gdma(qarr[:, 4096 * sl + 1024 * g:4096 * sl + 1024 * (g + 1)],
                     src, r=(hkey,),
                     w=rg("qarr", 4096 * sl + 1024 * g,
                          4096 * sl + 1024 * (g + 1)), tag="fold_ld")

        def compose_piece(hsrc, hkey, base, i):
            if i == 0:
                fold_load_set(hsrc, hkey, 0, 0)
                op("v", lambda v, base=base: v.tensor_copy(
                    varr[:, base:base + 4096], qarr[:, 0:4096]),
                   r=rg("qarr", 0, 4096), w=rg("varr", base, base + 4096))
                return
            sl = i % 4
            fold_load_set(hsrc, hkey, i, sl)
            for blk in range(32):
                pn = PSN[id(psA[blk % 2])]
                op("t", lambda t, base=base, blk=blk, sl=sl: t.matmul(
                    psA[blk % 2][:, 0:128],
                    qarr[:, 4096 * sl + 128 * blk:
                         4096 * sl + 128 * (blk + 1)],
                    varr[:, base + 128 * blk:base + 128 * (blk + 1)],
                    start=True, stop=True),
                    r=rg("qarr", 4096 * sl + 128 * blk,
                         4096 * sl + 128 * (blk + 1))
                    + rg("varr", base + 128 * blk, base + 128 * (blk + 1)),
                    w=pg(pn), tag="fmm")
                op("v", lambda v, base=base, blk=blk: v.tensor_copy(
                    varr[:, base + 128 * blk:base + 128 * (blk + 1)],
                    psA[blk % 2][:, 0:128]),
                    r=pg(pn),
                    w=rg("varr", base + 128 * blk, base + 128 * (blk + 1)),
                    tag="fev")

        MMCH = ((0, 512, 0), (512, 512, 1), (1024, 16, 2))
        PSETS = ((psA[0], psA[1], psB[0]), (psB[1], psO, psS))
        SRC0 = 4096

        def fold_apply(base, srcfn, srckey, nch, outfn, u):
            sl = u % 2
            so = SRC0 + 1152 * (u % 4)
            srcfn(u, so)
            for (o, wd, pi) in MMCH[:nch]:
                pb = PSETS[sl][pi]
                pn = PSN[id(pb)]
                op("t", lambda t, u=u, o=o, wd=wd, pb=pb, so=so: t.matmul(
                    pb[:, 0:wd], varr[:, base + 128 * u:base + 128 * (u + 1)],
                    trn[:, so + o:so + o + wd], start=True, stop=True),
                    r=rg("varr", base + 128 * u, base + 128 * (u + 1))
                    + rg("trn", so + o, so + o + wd),
                    w=pg(pn), tag="fmm")
            outfn(u)

        def ua1_src(blk, so):
            gdma(trn[:, so:so + 1040],
                 din["wexpT_aug"][128 * blk:128 * (blk + 1), :],
                 r=("wexpT",), w=rg("trn", so, so + 1040), tag="fold")

        def ua1_out(blk):
            oo = 2048 * (blk % 2)
            ps = PSETS[blk % 2]
            op("v", lambda v, oo=oo, ps=ps: v.tensor_copy(
                trn[:, oo:oo + 512], ps[0][:, 0:512]),
               r=pg(PSN[id(ps[0])]), w=rg("trn", oo, oo + 512))
            op("v", lambda v, oo=oo, ps=ps: v.tensor_copy(
                trn[:, oo + 512:oo + 1024], ps[1][:, 0:512]),
               r=pg(PSN[id(ps[1])]), w=rg("trn", oo + 512, oo + 1024))
            op("v", lambda v, oo=oo, ps=ps: v.tensor_copy(
                trn[:, oo + 1024:oo + 1040], ps[2][:, 0:16]),
               r=pg(PSN[id(ps[2])]), w=rg("trn", oo + 1024, oo + 1040))
            dma(_ap(zhb, 1040 * blk, [(32 * 1040, 128), (1, 1040)]),
                trn[:, oo:oo + 1040], r=rg("trn", oo, oo + 1040), w=("zhb",),
                tag="fold_sc")


        def ua2_src(t_, so):
            dma(trn[:, so:so + 1040], zhb[128 * t_:128 * (t_ + 1), :],
                r=("zhb",), w=rg("trn", so, so + 1040), tag="fold")

        def ua2_out(t_):
            oo = 1024 * (t_ % 2)
            to = 2048 + 1024 * (t_ % 2)
            ps = PSETS[t_ % 2]
            op("v", lambda v, oo=oo, ps=ps: v.tensor_copy(
                trn[:, oo:oo + 512], ps[0][:, 0:512]),
               r=pg(PSN[id(ps[0])]), w=rg("trn", oo, oo + 512))
            op("v", lambda v, oo=oo, ps=ps: v.tensor_copy(
                trn[:, oo + 512:oo + 1024], ps[1][:, 0:512]),
               r=pg(PSN[id(ps[1])]), w=rg("trn", oo + 512, oo + 1024))
            op("v", lambda v, t_=t_, ps=ps: v.tensor_copy(
                bvec[:, t_:t_ + 1], ps[2][:, 0:1]),
               r=pg(PSN[id(ps[2])]), w=rg("bvec", t_, t_ + 1))
            for g2 in range(2):
                for q4 in range(4):
                    ec = 4 * g2 + q4
                    op("t", lambda t, oo=oo, ec=ec, q4=q4: t.transpose(
                        psU[:, 128 * q4:128 * (q4 + 1)],
                        trn[:, oo + 128 * ec:oo + 128 * (ec + 1)], ident),
                        r=rg("trn", oo + 128 * ec, oo + 128 * (ec + 1))
                        + rg("cstb", 0, 128), w=("psUbank",), tag="ftr")
                op("v", lambda v, to=to, g2=g2: v.tensor_copy(
                    trn[:, to + 512 * g2:to + 512 * (g2 + 1)],
                    psU[:, 0:512]),
                    r=("psUbank",),
                    w=rg("trn", to + 512 * g2, to + 512 * (g2 + 1)))
            dma(weffh[t_, :, :], trn[:, to:to + 1024],
                r=rg("trn", to, to + 1024), w=("weffh",), tag="fold_st")


        def da1_src(t_, so):
            gdma(trn[:, so:so + 1024],
                 din["wcon_lo"][128 * t_:128 * (t_ + 1), :],
                 r=("wconlo",), w=rg("trn", so, so + 1024), tag="fold")

        def da1_out(t_):
            oo = 1024 * (t_ % 4)
            ps = PSETS[t_ % 2]
            op("v", lambda v, oo=oo, ps=ps: v.tensor_copy(
                trn[:, oo:oo + 512], ps[0][:, 0:512]),
               r=pg(PSN[id(ps[0])]), w=rg("trn", oo, oo + 512))
            op("v", lambda v, oo=oo, ps=ps: v.tensor_copy(
                trn[:, oo + 512:oo + 1024], ps[1][:, 0:512]),
               r=pg(PSN[id(ps[1])]), w=rg("trn", oo + 512, oo + 1024))
            dma(v1hb[128 * t_:128 * (t_ + 1), :], trn[:, oo:oo + 1024],
                r=rg("trn", oo, oo + 1024), w=("v1hb",), tag="fold_st")


        def da2_src(u, so):
            dma(trn[:, so:so + 1024],
                _ap(v1hb, 1024 * u, [(32 * 1024, 128), (1, 1024)]),
                r=("v1hb",), w=rg("trn", so, so + 1024), tag="fold")

        def da2_out(u):
            oo = 1024 * (u % 4)
            ps = PSETS[u % 2]
            op("v", lambda v, oo=oo, ps=ps: v.tensor_copy(
                trn[:, oo:oo + 512], ps[0][:, 0:512]),
               r=pg(PSN[id(ps[0])]), w=rg("trn", oo, oo + 512))
            op("v", lambda v, oo=oo, ps=ps: v.tensor_copy(
                trn[:, oo + 512:oo + 1024], ps[1][:, 0:512]),
               r=pg(PSN[id(ps[1])]), w=rg("trn", oo + 512, oo + 1024))
            for et in range(8):
                dst = _ap(wconch, et * 32 * 128 * 128 + u * 4096,
                          [(128, 32), (131072, 4), (1, 128)])
                dma(dst, trn[:, oo + 128 * et:oo + 128 * (et + 1)],
                    r=rg("trn", oo + 128 * et, oo + 128 * (et + 1)),
                    w=("wconch",), tag="fold_sc")

        if "fold" in _ph:
            for i in range(7):
                compose_piece(din["upl"], "upl", 0, i)
            pend = [(din["uph"], "uph", 4096, i) for i in range(5)]
            for blk in range(32):
                fold_apply(0, ua1_src, "wexpT", 3, ua1_out, blk)
                if blk % 6 == 3 and pend:
                    compose_piece(*pend.pop(0))
            while pend:
                compose_piece(*pend.pop(0))
            pend = [(din["dnh"], "dnh", 8192, i) for i in range(5)]
            for t_ in range(32):
                fold_apply(4096, ua2_src, "zhb", 3, ua2_out, t_)
                if t_ % 6 == 3 and pend:
                    compose_piece(*pend.pop(0))
            while pend:
                compose_piece(*pend.pop(0))
            op("v", lambda v: v.tensor_add(bvec[:, 0:32], bvec[:, 0:32],
                                           bvec[:, 32:64]),
               r=rg("bvec", 0, 64), w=rg("bvec", 0, 32))
            pend = [(din["dnl"], "dnl", 12288, i) for i in range(7)]
            for t_ in range(32):
                fold_apply(8192, da1_src, "wconlo", 2, da1_out, t_)
                if t_ % 4 == 2 and pend:
                    compose_piece(*pend.pop(0))
            while pend:
                compose_piece(*pend.pop(0))
            for u in range(32):
                fold_apply(12288, da2_src, "v1hb", 2, da2_out, u)

        def ln_stats(c0, c1, nt):
            op("v", lambda v: v.tensor_scalar(
                stat[:, 96:96 + nt], stat[:, c0 + 16:c0 + 16 + nt],
                1.0 / E, None, ALU.mult),
                r=rg("stat", c0 + 16, c0 + 16 + nt), w=rg("stat", 96, 96 + nt))
            op("v", lambda v: v.tensor_scalar(
                stat[:, 112:112 + nt], stat[:, c0:c0 + nt],
                1.0 / E, None, ALU.mult),
                r=rg("stat", c0, c0 + nt), w=rg("stat", 112, 112 + nt))
            op("v", lambda v: v.tensor_mul(
                stat[:, 128:128 + nt], stat[:, 96:96 + nt],
                stat[:, 96:96 + nt]),
                r=rg("stat", 96, 96 + nt), w=rg("stat", 128, 128 + nt))
            op("v", lambda v: v.tensor_sub(
                stat[:, 112:112 + nt], stat[:, 112:112 + nt],
                stat[:, 128:128 + nt]),
                r=rg("stat", 112, 112 + nt) + rg("stat", 128, 128 + nt),
                w=rg("stat", 112, 112 + nt))
            op("s", lambda s: s.activation(
                stat[:, 128:128 + nt], stat[:, 112:112 + nt], AF.Sqrt,
                bias=bvec[:, 88:89], scale=1.0),
                r=rg("stat", 112, 112 + nt) + rg("bvec", 88, 89),
                w=rg("stat", 128, 128 + nt))
            op("v", lambda v: v.reciprocal(
                stat[:, c1:c1 + nt], stat[:, 128:128 + nt]),
                r=rg("stat", 128, 128 + nt), w=rg("stat", c1, c1 + nt))
            op("v", lambda v: v.tensor_mul(
                stat[:, 144:144 + nt], stat[:, 96:96 + nt],
                stat[:, c1:c1 + nt]),
                r=rg("stat", 96, 96 + nt) + rg("stat", c1, c1 + nt),
                w=rg("stat", 144, 144 + nt))
            op("v", lambda v: v.tensor_scalar(
                stat[:, c1 + 16:c1 + 16 + nt], stat[:, 144:144 + nt],
                -1.0, None, ALU.mult),
                r=rg("stat", 144, 144 + nt),
                w=rg("stat", c1 + 16, c1 + 16 + nt))

        SQ = 8960
        for b in (range(BL) if "pre" in _ph else ()):
            for i in range(NTT):
                if b > 0:
                    dma(xtm[:, E * i:E * (i + 1)],
                        din["x"][b, 128 * i:128 * (i + 1), :],
                        r=("xin",), w=rg("xtm", E * i, E * (i + 1)),
                        tag="xld")

            for i in range(NTT):
                op("s", lambda s, i=i: s.activation(
                    trn[:, SQ:SQ + 1024], xtm[:, E * i:E * (i + 1)],
                    AF.Square, accum_out=stat[:, i:i + 1]),
                    r=rg("xtm", E * i, E * (i + 1)),
                    w=rg("trn", SQ, SQ + 1024) + rg("stat", i, i + 1),
                    tag="ln1")
                op("v", lambda v, i=i: v.reduce_sum(
                    stat[:, 16 + i:17 + i], xtm[:, E * i:E * (i + 1)],
                    axis=AX.X),
                    r=rg("xtm", E * i, E * (i + 1)),
                    w=rg("stat", 16 + i, 17 + i), tag="ln1")
            if "ln1" in _ph:
                ln_stats(0, 48, NTT)
            for i in (range(NTT) if "ln1" in _ph else ()):
                op("v", lambda v, i=i: v.tensor_scalar(
                    trn[:, SQ:SQ + 1024], xtm[:, E * i:E * (i + 1)],
                    stat[:, 48 + i:49 + i], stat[:, 64 + i:65 + i],
                    ALU.mult, ALU.add),
                    r=rg("xtm", E * i, E * (i + 1)) + rg("stat", 48 + i, 49 + i)
                    + rg("stat", 64 + i, 65 + i),
                    w=rg("trn", SQ, SQ + 1024), tag="ln1a")
                for g2 in range(2):
                    qw = ()
                    for q4 in range(4):
                        ec = 4 * g2 + q4
                        qw += rg("qarr", 2048 * ec + 128 * i,
                                 2048 * ec + 128 * (i + 1))
                        op("t", lambda t, ec=ec, q4=q4: t.transpose(
                            psU[:, 128 * q4:128 * (q4 + 1)],
                            trn[:, SQ + 128 * ec:SQ + 128 * (ec + 1)], ident),
                            r=rg("trn", SQ + 128 * ec, SQ + 128 * (ec + 1))
                            + rg("cstb", 0, 128), w=("psUbank",), tag="xhT")
                    op("v", lambda v, i=i, g2=g2: v.tensor_copy(
                        _ap(qarr, 2048 * 4 * g2 + 128 * i,
                            [(16384, 128), (2048, 4), (1, 128)]),
                        psU[:, 0:512].rearrange("p (c f) -> p c f", c=4)),
                        r=("psUbank",), w=qw, tag="xhTe")
            if b == 0:
                tap("xhfm", qarr[:, 0:1024], rg("qarr", 0, 1024))

            for j in (range(NP) if "kproj" in _ph else ()):
                ko = 1024 * (j % 2)
                dma(trn[:, ko:ko + 1024],
                    _ap(wk16, j * 128 * 1024,
                        [(1024, 128), (1, 1024)]),
                    r=("wk16",), w=rg("trn", ko, ko + 1024), tag="wkld")
                for kc in range(4):
                    pn = PSN[id(psA[kc % 2])]
                    for ec in range(8):
                        op("t", lambda t, kc=kc, ec=ec, ko=ko: t.matmul(
                            psA[kc % 2][:, 0:512],
                            trn[:, ko + 128 * ec:ko + 128 * (ec + 1)],
                            qarr[:, 2048 * ec + 512 * kc:
                                 2048 * ec + 512 * (kc + 1)],
                            start=(ec == 0), stop=(ec == 7)),
                            r=rg("trn", ko + 128 * ec, ko + 128 * (ec + 1))
                            + rg("qarr", 2048 * ec + 512 * kc,
                                 2048 * ec + 512 * (kc + 1)),
                            w=pg(pn), accum=(ec != 0), tag="kmm")
                    op("v", lambda v, j=j, kc=kc: v.tensor_copy(
                        karr[:, 2048 * j + 512 * kc:2048 * j + 512 * (kc + 1)],
                        psA[kc % 2][:, 0:512]),
                        r=pg(pn),
                        w=rg("karr", 2048 * j + 512 * kc,
                             2048 * j + 512 * (kc + 1)), tag="kev")
            if b == 0:
                tap("kall", karr[:, 0:1024], rg("karr", 0, 1024))

            for ec in (range(8) if "vproj" in _ph else ()):
                dma(trn[:, 1024 * ec:1024 * (ec + 1)],
                    wv16[128 * ec:128 * (ec + 1), :],
                    r=("wv16",),
                    w=rg("trn", 1024 * ec, 1024 * (ec + 1)),
                    tag="wvld")
            if "vproj" in _ph:
                op("v", lambda v: v.memset(
                    _ap(varr, 64, [(16640, 128), (65, 256), (1, 1)]), 1.0),
                    w=rg("varr", 0, 16640), tag="vones")
            for kt in (range(NKT) if "vproj" in _ph else ()):
                for fc in range(2):
                    pn = PSN[id(psB[fc])]
                    for ec in range(8):
                        op("t", lambda t, kt=kt, fc=fc, ec=ec: t.matmul(
                            psB[fc][:, 0:512],
                            qarr[:, 2048 * ec + 128 * kt:
                                 2048 * ec + 128 * (kt + 1)],
                            trn[:, 1024 * ec + 512 * fc:
                                 1024 * ec + 512 * (fc + 1)],
                            start=(ec == 0), stop=(ec == 7)),
                            r=rg("qarr", 2048 * ec + 128 * kt,
                                 2048 * ec + 128 * (kt + 1))
                            + rg("trn", 1024 * ec + 512 * fc,
                                 1024 * ec + 512 * (fc + 1)),
                            w=pg(pn), accum=(ec != 0), tag="vmm")
                    op("v", lambda v, kt=kt, fc=fc: v.tensor_copy(
                        _ap(varr, 1040 * kt + 520 * fc,
                            [(16640, 128), (65, 8), (1, 64)]),
                        psB[fc][:, 0:512].rearrange("p (h d) -> p h d", h=8)),
                        r=pg(pn),
                        w=rg("varr", 1040 * kt + 520 * fc,
                             1040 * kt + 520 * (fc + 1)), tag="vev")
            if b == 0:
                tap("vtm", _ap(varr, 0, [(16640, 128), (65, 16), (1, 64)]),
                    rg("varr", 0, 1040))

            for j in (range(NP) if "qproj" in _ph else ()):
                ko = 3072 * (j % 2)
                dma(trn[:, ko:ko + 1024],
                    _ap(wq16, j * 128 * 1024,
                        [(1024, 128), (1, 1024)]),
                    r=("wq16",), w=rg("trn", ko, ko + 1024), tag="wqld")
                qo = 1024 + 3584 * (j % 2)
                for kc in range(4):
                    pn = PSN[id(psA[kc % 2])]
                    for ec in range(8):
                        op("t", lambda t, kc=kc, ec=ec, ko=ko: t.matmul(
                            psA[kc % 2][:, 0:512],
                            trn[:, ko + 128 * ec:ko + 128 * (ec + 1)],
                            qarr[:, 2048 * ec + 512 * kc:
                                 2048 * ec + 512 * (kc + 1)],
                            start=(ec == 0), stop=(ec == 7)),
                            r=rg("trn", ko + 128 * ec, ko + 128 * (ec + 1))
                            + rg("qarr", 2048 * ec + 512 * kc,
                                 2048 * ec + 512 * (kc + 1)),
                            w=pg(pn), accum=(ec != 0), tag="qmm")
                    op("v", lambda v, kc=kc, qo=qo: v.tensor_copy(
                        trn[:, qo + 512 * kc:qo + 512 * (kc + 1)],
                        psA[kc % 2][:, 0:512]),
                        r=pg(pn),
                        w=rg("trn", qo + 512 * kc, qo + 512 * (kc + 1)),
                        tag="qev")
                dma(q16[j, :, :], trn[:, qo:qo + 2048],
                    r=rg("trn", qo, qo + 2048), w=(f"q16_{j}",), tag="qst")
            for qc in (range(NQC) if "att" in _ph else ()):
                qb = QC * qc
                qbase = 4096 * (qc % 2)
                for j in range(NP):
                    dma(qarr[:, qbase + 512 * j:qbase + 512 * (j + 1)],
                        q16[j, :, qb:qb + 512], r=(f"q16_{j}",),
                        w=rg("qarr", qbase + 512 * j, qbase + 512 * (j + 1)),
                        tag="qld")
                if b == 0 and qc == 0:
                    tap("qall", qarr[:, 0:1024], rg("qarr", 0, 1024))
                for j in range(NP):
                    rodd = j % 2
                    rbuf, rb0, rnm = ((qarr, 8192, "qarr") if rodd
                                      else (trn, 0, "trn"))
                    for hh in range(2):
                        dma(rbuf[:, rb0 + RW * hh:rb0 + RW * (hh + 1)],
                            r16[2 * j + hh, :, qb + 127:qb + 127 + RW],
                            r=("r16",),
                            w=rg(rnm, rb0 + RW * hh, rb0 + RW * (hh + 1)),
                            tag="rwin")
                    kreg = rg("karr", 2048 * j, 2048 * (j + 1))
                    qreg = rg("qarr", qbase + 512 * j, qbase + 512 * (j + 1))

                    def sc_ops(kt, j=j, qb=qb, qbase=qbase,
                               kreg=kreg, qreg=qreg):
                        par = kt % 2
                        pnA, pnB = PSN[id(psA[par])], PSN[id(psB[par])]
                        eoA = 4864 + 512 * (kt % 4)
                        eoB = 6912 + 512 * (kt % 4)
                        op("t", lambda t, j=j, kt=kt, par=par,
                           qbase=qbase: t.matmul(
                            psA[par][:, 0:512],
                            karr[0:64, 2048 * j + 128 * kt:
                                 2048 * j + 128 * (kt + 1)],
                            qarr[0:64, qbase + 512 * j:qbase + 512 * (j + 1)],
                            start=True, stop=True, tile_position=(0, 0)),
                            r=kreg + qreg, w=pg(pnA), tag="sc")
                        op("t", lambda t, j=j, kt=kt, par=par,
                           qbase=qbase: t.matmul(
                            psB[par][:, 0:512],
                            karr[64:128, 2048 * j + 128 * kt:
                                 2048 * j + 128 * (kt + 1)],
                            qarr[64:128, qbase + 512 * j:
                                 qbase + 512 * (j + 1)],
                            start=True, stop=True, tile_position=(64, 0)),
                            r=kreg + qreg, w=pg(pnB), tag="sc")
                        op("s", lambda s, eoA=eoA, par=par: s.activation(
                            trn[:, eoA:eoA + 512], psA[par][:, 0:512],
                            AF.Exp, scale=0.03125),
                            r=pg(pnA), w=rg("trn", eoA, eoA + 512), tag="exp")
                        op("s", lambda s, eoB=eoB, par=par: s.activation(
                            trn[:, eoB:eoB + 512], psB[par][:, 0:512],
                            AF.Exp, scale=0.03125),
                            r=pg(pnB), w=rg("trn", eoB, eoB + 512), tag="exp")

                    def avrb_ops(kt, j=j, rbuf=rbuf, rb0=rb0, rnm=rnm):
                        eoA = 4864 + 512 * (kt % 4)
                        eoB = 6912 + 512 * (kt % 4)
                        vb = 1040 * kt + 130 * j
                        vregA = rg("varr", vb, vb + 65)
                        vregB = rg("varr", vb + 65, vb + 130)
                        op("t", lambda t, kt=kt, vb=vb, eoA=eoA: t.matmul(
                            psO[0:65, :], varr[:, vb:vb + 65],
                            trn[:, eoA:eoA + 512],
                            start=(kt == 0), stop=(kt == NKT - 1),
                            tile_position=(0, 0)),
                            r=vregA + rg("trn", eoA, eoA + 512),
                            w=pg("psO", 0, 96), accum=(kt != 0), tag="av")
                        op("t", lambda t, kt=kt, vb=vb, eoB=eoB: t.matmul(
                            psS[0:65, :], varr[:, vb + 65:vb + 130],
                            trn[:, eoB:eoB + 512],
                            start=(kt == 0), stop=(kt == NKT - 1),
                            tile_position=(0, 0)),
                            r=vregB + rg("trn", eoB, eoB + 512),
                            w=pg("psS", 0, 96), accum=(kt != 0), tag="av")
                        roff = rb0 + 1920 - 128 * kt
                        op("t", lambda t, kt=kt, vb=vb, roff=roff,
                           rbuf=rbuf: t.matmul(
                            psC[0:64, :], varr[:, vb:vb + 64],
                            rbuf[:, roff:roff + 512],
                            start=(kt == 0), stop=(kt == NKT - 1),
                            tile_position=(0, 0)),
                            r=vregA + rg(rnm, roff, roff + 512),
                            w=pg("psC", 0, 64), accum=(kt != 0), tag="rb")
                        op("t", lambda t, kt=kt, vb=vb, roff=roff,
                           rbuf=rbuf: t.matmul(
                            psC[64:128, :], varr[:, vb + 65:vb + 129],
                            rbuf[:, RW + roff:RW + roff + 512],
                            start=(kt == 0), stop=(kt == NKT - 1),
                            tile_position=(0, 64), skip_group_check=True),
                            r=vregB + rg(rnm, RW + roff, RW + roff + 512),
                            w=pg("psC", 64, 128), accum=(kt != 0), tag="rb")

                    sc_ops(0)
                    sc_ops(1)
                    for kt in range(NKT):
                        avrb_ops(kt)
                        if kt + 2 < NKT:
                            sc_ops(kt + 2)

                    op("v", lambda v: v.reciprocal(f32a[64:65, 1024:1536],
                                                   psO[64:65, :]),
                       r=pg("psO", 64, 96), w=rg("f32a", 1024, 1536), tag="as")
                    op("v", lambda v: v.reciprocal(f32a[64:65, 1536:2048],
                                                   psS[64:65, :]),
                       r=pg("psS", 64, 96), w=rg("f32a", 1536, 2048), tag="as")
                    op("t", lambda t: t.matmul(
                        psA[0][0:64, :], cst[64:65, 128:192],
                        f32a[64:65, 1024:1536], start=True, stop=True,
                        tile_position=(64, 0)),
                        r=rg("f32a", 1024, 1536) + rg("cst", 128, 192),
                        w=pg("psA0", 0, 64), tag="as")
                    op("t", lambda t: t.matmul(
                        psA[0][64:128, :], cst[64:65, 128:192],
                        f32a[64:65, 1536:2048], start=True, stop=True,
                        tile_position=(64, 64), skip_group_check=True),
                        r=rg("f32a", 1536, 2048) + rg("cst", 128, 192),
                        w=pg("psA0", 64, 128), tag="as")
                    op("v", lambda v: v.tensor_copy(trnf[:, 4480:4992],
                                                    psA[0][:, 0:512]),
                       r=pg("psA0"), w=rg("trn", 8960, 9984), tag="as")
                    op("v", lambda v: v.tensor_tensor(
                        f32a[0:64, 0:512], psO[0:64, 0:512],
                        trnf[0:64, 4480:4992], ALU.mult),
                       r=pg("psO", 0, 64) + rg("trn", 8960, 9984),
                       w=rg("f32a", 0, 512), tag="as")
                    op("v", lambda v: v.tensor_tensor(
                        f32a[64:128, 0:512], psS[0:64, 0:512],
                        trnf[64:128, 4480:4992], ALU.mult),
                       r=pg("psS", 0, 64) + rg("trn", 8960, 9984),
                       w=rg("f32a", 0, 512), tag="as")
                    op("v", lambda v: v.tensor_tensor(
                        f32a[:, 512:1024], f32a[:, 0:512], psC[:, 0:512],
                        ALU.add),
                       r=rg("f32a", 0, 512) + pg("psC"),
                       w=rg("f32a", 512, 1024), tag="as")
                    for c in range(4):
                        op("t", lambda t, c=c: t.transpose(
                            psO[:, 128 * c:128 * (c + 1)],
                            f32a[:, 512 + 128 * c:512 + 128 * (c + 1)], eyef),
                            r=rg("f32a", 512 + 128 * c, 512 + 128 * (c + 1))
                            + rg("cst", 0, 128), w=pg("psO"), tag="asT")
                    aosegs = tuple(f"aotm:{(1024 * c + 128 * j) // 128}"
                                   for c in range(4))
                    op("v", lambda v, j=j: v.tensor_copy(
                        _ap(aotm, 128 * j, [(4096, 128), (1024, 4), (1, 128)]),
                        psO[:, 0:512].rearrange("p (c f) -> p c f", c=4)),
                        r=pg("psO"), w=aosegs, tag="asE")

                dma(f32a[:, 0:1024], din["gbc"][0, :, :], r=("gbc",),
                    w=rg("f32a", 0, 1024), tag="gbld")
                dma(trnf[:, 0:1024], din["gbc"][1, :, :], r=("gbc",),
                    w=rg("trn", 0, 2048), tag="gbld")
                for c in range(4):
                    aoreg = rg("aotm", 1024 * c, 1024 * (c + 1))
                    op("s", lambda s, c=c: s.activation(
                        trn[:, SQ:SQ + 1024], aotm[:, 1024 * c:1024 * (c + 1)],
                        AF.Square, accum_out=stat[:, c:c + 1]),
                        r=aoreg,
                        w=rg("trn", SQ, SQ + 1024) + rg("stat", c, c + 1),
                        tag="lnA")
                    op("v", lambda v, c=c: v.reduce_sum(
                        stat[:, 16 + c:17 + c],
                        aotm[:, 1024 * c:1024 * (c + 1)], axis=AX.X),
                        r=aoreg, w=rg("stat", 16 + c, 17 + c), tag="lnA")
                ln_stats(0, 48, 4)
                for c in range(4):
                    ti = 4 * qc + c
                    aoreg = rg("aotm", 1024 * c, 1024 * (c + 1))
                    op("v", lambda v, c=c: v.tensor_scalar(
                        f32a[:, 1024:2048], aotm[:, 1024 * c:1024 * (c + 1)],
                        stat[:, 48 + c:49 + c], stat[:, 64 + c:65 + c],
                        ALU.mult, ALU.add),
                        r=aoreg + rg("stat", 48 + c, 49 + c)
                        + rg("stat", 64 + c, 65 + c),
                        w=rg("f32a", 1024, 2048), tag="lnA2")
                    op("v", lambda v, c=c: v.tensor_mul(
                        aotm[:, 1024 * c:1024 * (c + 1)], f32a[:, 1024:2048],
                        f32a[:, 0:1024]),
                        r=rg("f32a", 1024, 2048) + rg("f32a", 0, 1024),
                        w=aoreg, tag="lnA2")
                    op("v", lambda v, c=c: v.tensor_add(
                        aotm[:, 1024 * c:1024 * (c + 1)],
                        aotm[:, 1024 * c:1024 * (c + 1)], trnf[:, 0:1024]),
                        r=aoreg + rg("trn", 0, 2048), w=aoreg, tag="lnA2")
                    op("s", lambda s, c=c: s.activation(
                        trn[:, SQ:SQ + 1024], aotm[:, 1024 * c:1024 * (c + 1)],
                        AF.Square, accum_out=stat[:, 32 + c:33 + c]),
                        r=aoreg,
                        w=rg("trn", SQ, SQ + 1024) + rg("stat", 32 + c, 33 + c),
                        tag="lnB")
                    op("v", lambda v, c=c: v.reduce_sum(
                        stat[:, 20 + c:21 + c],
                        aotm[:, 1024 * c:1024 * (c + 1)], axis=AX.X),
                        r=aoreg, w=rg("stat", 20 + c, 21 + c), tag="lnB")
                op("v", lambda v: v.tensor_scalar(
                    stat[:, 96:100], stat[:, 20:24], 1.0 / E, None, ALU.mult),
                    r=rg("stat", 20, 24), w=rg("stat", 96, 100))
                op("v", lambda v: v.tensor_scalar(
                    stat[:, 112:116], stat[:, 32:36], 1.0 / E, None, ALU.mult),
                    r=rg("stat", 32, 36), w=rg("stat", 112, 116))
                op("v", lambda v: v.tensor_mul(
                    stat[:, 128:132], stat[:, 96:100], stat[:, 96:100]),
                    r=rg("stat", 96, 100), w=rg("stat", 128, 132))
                op("v", lambda v: v.tensor_sub(
                    stat[:, 112:116], stat[:, 112:116], stat[:, 128:132]),
                    r=rg("stat", 112, 116) + rg("stat", 128, 132),
                    w=rg("stat", 112, 116))
                op("s", lambda s: s.activation(
                    stat[:, 128:132], stat[:, 112:116], AF.Sqrt,
                    bias=bvec[:, 88:89], scale=1.0),
                    r=rg("stat", 112, 116) + rg("bvec", 88, 89),
                    w=rg("stat", 128, 132))
                op("v", lambda v: v.reciprocal(stat[:, 80:84],
                                               stat[:, 128:132]),
                   r=rg("stat", 128, 132), w=rg("stat", 80, 84))
                op("v", lambda v: v.tensor_mul(
                    stat[:, 144:148], stat[:, 96:100], stat[:, 80:84]),
                    r=rg("stat", 96, 100) + rg("stat", 80, 84),
                    w=rg("stat", 144, 148))
                op("v", lambda v: v.tensor_scalar(
                    stat[:, 84:88], stat[:, 144:148], -1.0, None, ALU.mult),
                    r=rg("stat", 144, 148), w=rg("stat", 84, 88))
                for c in range(4):
                    ti = 4 * qc + c
                    aoreg = rg("aotm", 1024 * c, 1024 * (c + 1))
                    xreg = rg("xtm", E * ti, E * (ti + 1))
                    op("v", lambda v, c=c: v.tensor_scalar(
                        f32a[:, 1024:2048], aotm[:, 1024 * c:1024 * (c + 1)],
                        stat[:, 80 + c:81 + c], stat[:, 84 + c:85 + c],
                        ALU.mult, ALU.add),
                        r=aoreg + rg("stat", 80 + c, 81 + c)
                        + rg("stat", 84 + c, 85 + c),
                        w=rg("f32a", 1024, 2048), tag="lnB2")
                    op("v", lambda v, ti=ti: v.tensor_add(
                        xtm[:, E * ti:E * (ti + 1)],
                        xtm[:, E * ti:E * (ti + 1)], f32a[:, 1024:2048]),
                        r=xreg + rg("f32a", 1024, 2048), w=xreg, tag="resA")
                if b == 0 and qc == 0:
                    tap("x1", xtm[:, 0:1024], rg("xtm", 0, 1024))

            for tc in (range(NQC) if "ffn" in _ph else ()):
                for c in range(4):
                    ti = 4 * tc + c
                    xreg = rg("xtm", E * ti, E * (ti + 1))
                    op("s", lambda s, ti=ti, c=c: s.activation(
                        trn[:, SQ:SQ + 1024], xtm[:, E * ti:E * (ti + 1)],
                        AF.Square, accum_out=stat[:, c:c + 1]),
                        r=xreg,
                        w=rg("trn", SQ, SQ + 1024) + rg("stat", c, c + 1),
                        tag="ln3")
                    op("v", lambda v, ti=ti, c=c: v.reduce_sum(
                        stat[:, 16 + c:17 + c], xtm[:, E * ti:E * (ti + 1)],
                        axis=AX.X),
                        r=xreg, w=rg("stat", 16 + c, 17 + c), tag="ln3")
                ln_stats(0, 48, 4)
                for c in range(4):
                    ti = 4 * tc + c
                    xreg = rg("xtm", E * ti, E * (ti + 1))
                    op("v", lambda v, ti=ti, c=c: v.tensor_scalar(
                        trn[:, 4096:5120], xtm[:, E * ti:E * (ti + 1)],
                        stat[:, 48 + c:49 + c], stat[:, 64 + c:65 + c],
                        ALU.mult, ALU.add),
                        r=xreg + rg("stat", 48 + c, 49 + c)
                        + rg("stat", 64 + c, 65 + c),
                        w=rg("trn", 4096, 5120), tag="ln3a")
                    for g2 in range(2):
                        qw = ()
                        for q4 in range(4):
                            ec = 4 * g2 + q4
                            qw += rg("trn", 512 * ec + 128 * c,
                                     512 * ec + 128 * (c + 1))
                            op("t", lambda t, ec=ec, q4=q4: t.transpose(
                                psU[:, 128 * q4:128 * (q4 + 1)],
                                trn[:, 4096 + 128 * ec:4096 + 128 * (ec + 1)],
                                ident),
                                r=rg("trn", 4096 + 128 * ec,
                                     4096 + 128 * (ec + 1))
                                + rg("cstb", 0, 128), w=("psUbank",),
                                tag="x3T")
                        op("v", lambda v, c=c, g2=g2: v.tensor_copy(
                            _ap(trn, 512 * 4 * g2 + 128 * c,
                                [(9984, 128), (512, 4), (1, 128)]),
                            psU[:, 0:512].rearrange("p (q f) -> p q f", q=4)),
                            r=("psUbank",), w=qw, tag="x3Te")

                for ht in range(32):
                    wo = 5120 + 1024 * (ht % 2)
                    dma(trn[:, wo:wo + 1024],
                        _ap(weffh, ht * 128 * 1024,
                            [(1024, 128), (1, 1024)]),
                        r=("weffh",), w=rg("trn", wo, wo + 1024), tag="wefld")
                    pn = PSN[id(psA[ht % 2])]
                    for ec in range(8):
                        op("t", lambda t, ht=ht, ec=ec, wo=wo: t.matmul(
                            psA[ht % 2][:, 0:512],
                            trn[:, wo + 128 * ec:wo + 128 * (ec + 1)],
                            trn[:, 512 * ec:512 * (ec + 1)],
                            start=(ec == 0), stop=(ec == 7)),
                            r=rg("trn", wo + 128 * ec, wo + 128 * (ec + 1))
                            + rg("trn", 512 * ec, 512 * (ec + 1)),
                            w=pg(pn), accum=(ec != 0), tag="hmm")
                    op("s", lambda s, ht=ht: s.activation(
                        karr[:, 512 * ht:512 * (ht + 1)],
                        psA[ht % 2][:, 0:512], AF.Gelu,
                        bias=bvec[:, ht:ht + 1], scale=1.0),
                        r=pg(pn) + rg("bvec", ht, ht + 1),
                        w=rg("karr", 512 * ht, 512 * (ht + 1)), tag="gelu")
                if b == 0 and tc == 0:
                    tap("hid", karr[:, 0:1024], rg("karr", 0, 1024))

                for et in range(8):
                    wo = 4096 * (et % 2)
                    dma(trn[:, wo:wo + 4096],
                        _ap(wconch, et * 32 * 128 * 128,
                            [(4096, 128), (1, 4096)]),
                        r=("wconch",), w=rg("trn", wo, wo + 4096), tag="wccld")
                    pn = PSN[id(psB[et % 2])]
                    for ht in range(32):
                        op("t", lambda t, et=et, ht=ht, wo=wo: t.matmul(
                            psB[et % 2][:, 0:512],
                            trn[:, wo + 128 * ht:wo + 128 * (ht + 1)],
                            karr[:, 512 * ht:512 * (ht + 1)],
                            start=(ht == 0), stop=(ht == 31)),
                            r=rg("trn", wo + 128 * ht, wo + 128 * (ht + 1))
                            + rg("karr", 512 * ht, 512 * (ht + 1)),
                            w=pg(pn), accum=(ht != 0), tag="ymm")
                    op("v", lambda v, et=et: v.tensor_scalar(
                        aotm[:, 512 * et:512 * (et + 1)],
                        psB[et % 2][:, 0:512], bvec[:, 80 + et:81 + et],
                        None, ALU.add),
                        r=pg(pn) + rg("bvec", 80 + et, 81 + et),
                        w=rg("aotm", 512 * et, 512 * (et + 1)), tag="yev")
                if b == 0 and tc == 0:
                    tap("yfm", aotm[:, 0:1024], rg("aotm", 0, 1024))

                for c in range(4):
                    for g2 in range(2):
                        for q4 in range(4):
                            et = 4 * g2 + q4
                            op("t", lambda t, et=et, c=c, q4=q4: t.transpose(
                                psO[:, 128 * q4:128 * (q4 + 1)],
                                aotm[:, 512 * et + 128 * c:
                                     512 * et + 128 * (c + 1)], eyef),
                                r=rg("aotm", 512 * et + 128 * c,
                                     512 * et + 128 * (c + 1))
                                + rg("cst", 0, 128),
                                w=pg("psO"), tag="yT")
                        ko = 8192 + 2048 * c + 1024 * g2
                        op("v", lambda v, c=c, g2=g2: v.tensor_copy(
                            kf[:, 4096 + 1024 * c + 512 * g2:
                               4096 + 1024 * c + 512 * (g2 + 1)],
                            psO[:, 0:512]),
                            r=pg("psO"),
                            w=rg("karr", ko, ko + 1024), tag="yTe")
                for c in range(4):
                    kr = rg("karr", 8192 + 2048 * c, 8192 + 2048 * (c + 1))
                    op("s", lambda s, c=c: s.activation(
                        trn[:, SQ:SQ + 1024],
                        kf[:, 4096 + 1024 * c:4096 + 1024 * (c + 1)],
                        AF.Square, accum_out=stat[:, c:c + 1]),
                        r=kr,
                        w=rg("trn", SQ, SQ + 1024) + rg("stat", c, c + 1),
                        tag="ln4")
                    op("v", lambda v, c=c: v.reduce_sum(
                        stat[:, 16 + c:17 + c],
                        kf[:, 4096 + 1024 * c:4096 + 1024 * (c + 1)],
                        axis=AX.X),
                        r=kr, w=rg("stat", 16 + c, 17 + c), tag="ln4")
                ln_stats(0, 48, 4)
                for c in range(4):
                    ti = 4 * tc + c
                    kr = rg("karr", 8192 + 2048 * c, 8192 + 2048 * (c + 1))
                    op("v", lambda v, c=c: v.tensor_scalar(
                        f32a[:, 1024:2048],
                        kf[:, 4096 + 1024 * c:4096 + 1024 * (c + 1)],
                        stat[:, 48 + c:49 + c], stat[:, 64 + c:65 + c],
                        ALU.mult, ALU.add),
                        r=kr + rg("stat", 48 + c, 49 + c)
                        + rg("stat", 64 + c, 65 + c),
                        w=rg("f32a", 1024, 2048), tag="ln4a")
                    xreg = rg("xtm", E * ti, E * (ti + 1))
                    op("v", lambda v, ti=ti: v.tensor_add(
                        xtm[:, E * ti:E * (ti + 1)],
                        xtm[:, E * ti:E * (ti + 1)], f32a[:, 1024:2048]),
                        r=xreg + rg("f32a", 1024, 2048), w=xreg, tag="resF")
                    dma(out_h[b, 128 * ti:128 * (ti + 1), :],
                        xtm[:, E * ti:E * (ti + 1)], r=xreg,
                        w=(f"out{b}_{ti}",), tag="outst")

        if "ffn" not in _ph or "pre" not in _ph:
            for ti in range(2):
                dma(xtm[:, E * ti:E * (ti + 1)],
                    din["x"][0, 128 * ti:128 * (ti + 1), :],
                    r=("xin",), w=rg("xtm", E * ti, E * (ti + 1)))
                dma(out_h[0, 128 * ti:128 * (ti + 1), :],
                    xtm[:, E * ti:E * (ti + 1)],
                    r=rg("xtm", E * ti, E * (ti + 1)),
                    w=(f"outz{ti}",), tag="outst")
        ir.finalize(nc, ctx)
    return nc, tap_h, ir


_CACHED = {}


def kernel(**inputs):
    x = np.asarray(inputs["L0_attn_x"], np.float32)
    prep = _host_prep(inputs)
    if "nc" not in _CACHED:
        _CACHED["nc"] = build_program()[0]
    nc = _CACHED["nc"]
    in_maps = []
    for c in range(NCORES):
        m = {"x": np.ascontiguousarray(x[BL * c:BL * (c + 1)])}
        m.update(prep)
        in_maps.append(m)
    res = run_bass_kernel_spmd(nc, in_maps, core_ids=list(range(NCORES)))
    return np.concatenate([r["out"] for r in res.results], axis=0)


# version 13
# speedup vs baseline: 1.0159x; 1.0006x over previous
import numpy as np

import concourse.bass as bass
import concourse.mybir as mybir
from concourse.bass_utils import run_bass_kernel_spmd
from contextlib import ExitStack

F32 = mybir.dt.float32
BF16 = mybir.dt.bfloat16
AF = mybir.ActivationFunctionType
ALU = mybir.AluOpType
AX = mybir.AxisListType

NCORES = 8
BL = 2
S = 2048
E = 1024
H = 16
NP = 8
DHID = 4096
NKT = 16
QC = 512
NQC = 4
NTT = 16
LN_EPS = 1e-5
RW = 2432


class Buf:
    __slots__ = ("w", "rs")

    def __init__(self):
        self.w = None
        self.rs = []


import os as _os_
NQLANES = int(_os_.environ.get("NQLANES", "16"))


class IR:
    ENGS = ("t", "s", "v", "g") + tuple(f"q{k}" for k in range(NQLANES))

    def __init__(self):
        self.ops = []
        self.eng_ops = {e: [] for e in self.ENGS}
        self.bufs = {}
        self.qcount = 0
        self.qorder = []

    def _b(self, key):
        b = self.bufs.get(key)
        if b is None:
            b = self.bufs[key] = Buf()
        return b

    def op(self, eng, fn, r=(), w=(), accum=False, tag=""):
        i = len(self.ops)
        qlane_prev = None
        if eng == "q":
            lane = self.qcount % NQLANES
            self.qcount += 1
            eng = f"q{lane}"
            if self.eng_ops[eng]:
                qlane_prev = self.eng_ops[eng][-1]
            self.qorder.append(i)
        elif eng == "g":
            if self.eng_ops["g"]:
                qlane_prev = self.eng_ops["g"][-1]
        reads = [self._b(k) for k in r]
        writes = [self._b(k) for k in w]
        wkeys = tuple(w)
        deps = set()
        if qlane_prev is not None:
            deps.add(qlane_prev)
        for b in reads:
            if b.w is not None:
                deps.add(b.w)
        for b in writes:
            if b.w is not None:
                if not (accum and self.ops[b.w]["acc"] is not None
                        and set(self.ops[b.w]["acc"]) >= set(wkeys)):
                    deps.add(b.w)
            deps.update(b.rs)
        deps.discard(i)
        self.ops.append({"eng": eng, "fn": fn, "deps": deps, "tag": tag,
                         "acc": wkeys if accum else None,
                         "ord": len(self.eng_ops[eng])})
        self.eng_ops[eng].append(i)
        for b in reads:
            b.rs.append(i)
        for b in writes:
            b.w = i
            b.rs = []
        return i

    def finalize(self, nc, ctx):
        ops = self.ops
        n = len(ops)
        NEI = {e: k for k, e in enumerate(self.ENGS)}
        NE = len(self.ENGS)
        last = {e: None for e in self.ENGS}
        kept = [None] * n
        clocks = [None] * n
        qclock = [-1] * NE
        for i, o in enumerate(ops):
            e = o["eng"]
            is_q = e.startswith("q")
            if is_q:
                base = list(qclock)
            else:
                prev = last[e]
                base = list(clocks[prev]) if prev is not None else [-1] * NE
            kd = []
            for d in sorted(o["deps"], key=lambda d: -ops[d]["ord"]):
                de = ops[d]["eng"]
                if base[NEI[de]] >= ops[d]["ord"]:
                    continue
                kd.append(d)
                dc = clocks[d]
                for k in range(NE):
                    if dc[k] > base[k]:
                        base[k] = dc[k]
                if ops[d]["ord"] > base[NEI[de]]:
                    base[NEI[de]] = ops[d]["ord"]
            kept[i] = kd
            if is_q:
                qclock = base
                cc = list(base)
                if o["ord"] > cc[NEI[e]]:
                    cc[NEI[e]] = o["ord"]
                clocks[i] = cc
            else:
                clocks[i] = base
            last[e] = i
        need = [False] * n
        for i in range(n):
            for d in kept[i]:
                need[d] = True
        for i in self.qorder:
            need[i] = True
        for i in self.eng_ops["g"]:
            need[i] = True
        semval = [0] * n
        for e in self.ENGS:
            step = 16 if (e.startswith("q") or e == "g") else 1
            c = 0
            for i in self.eng_ops[e]:
                if need[i]:
                    c += step
                semval[i] = c
            assert c < 60000, (e, c)
        sems = {e: ctx.enter_context(nc.semaphore("sem_" + e))
                for e in self.ENGS}

        def emit_ops(engine, idxs):
            for i in idxs:
                o = ops[i]
                for d in kept[i]:
                    engine.wait_ge(sems[ops[d]["eng"]], semval[d])
                ins = o["fn"](engine)
                if need[i]:
                    e = o["eng"]
                    ins.then_inc(sems[e],
                                 16 if (e.startswith("q") or e == "g") else 1)

        def emit(e):
            return lambda engine: emit_ops(engine, self.eng_ops[e])

        def emit_q(engine):
            emit_ops(engine, self.qorder)
            for e in self.ENGS:
                if e.startswith("q") and self.eng_ops[e]:
                    engine.wait_ge(sems[e], semval[self.eng_ops[e][-1]])

        block = ctx.enter_context(nc.Block())
        block.tensor(emit("t"))
        block.scalar(emit("s"))
        block.vector(emit("v"))
        block.gpsimd(emit("g"))
        block.sync(emit_q)
        self.stats = {
            "n_ops": n,
            "n_waits": sum(len(k) for k in kept),
            "n_incs": sum(1 for x in need if x),
        }


GRAN = {"xtm": 1024, "qarr": 512, "karr": 512, "varr": 128, "aotm": 128,
        "trn": 128, "f32a": 128, "stat": 1, "bvec": 4, "cst": 64,
        "cstb": 33, "gbrow": 512}


def rg(name, lo, hi):
    g = GRAN[name]
    return tuple(f"{name}:{i}" for i in range(lo // g, (hi + g - 1) // g))


def pg(name, plo=0, phi=128):
    return tuple(f"{name}#{i}" for i in range(plo // 32, (phi + 31) // 32))


def _low_blocks(w, stage, transpose):
    s = 1 << stage
    out = np.zeros((32, 128, 128), np.float32)
    i = np.arange(DHID)
    ii = i[(i & s) == 0]
    jj = ii + s
    q = ii >> (stage + 1)
    t = ii & (s - 1)
    k = q * s + t
    w4 = np.asarray(w[stage], np.float32).reshape(-1, 2, 2)
    a, b, c, d = w4[k, 0, 0], w4[k, 0, 1], w4[k, 1, 0], w4[k, 1, 1]
    if transpose:
        b, c = c, b
    blk, li, lj = ii // 128, ii % 128, jj % 128
    out[blk, li, li] = a
    out[blk, li, lj] = b
    out[blk, lj, li] = c
    out[blk, lj, lj] = d
    return out


def _high_blocks(w, stage, transpose):
    st = 1 << stage
    sh = st >> 7
    out = np.zeros((32, 128, 128), np.float32)
    w4 = np.asarray(w[stage], np.float32).reshape(-1, 2, 2)
    hi = np.arange(32)
    h1 = hi[(hi & sh) == 0]
    h2 = h1 + sh
    for lo in range(128):
        tt, o = lo >> 2, 32 * (lo & 3)
        i = h1 * 128 + lo
        q = i >> (stage + 1)
        t = i & (st - 1)
        k = q * st + t
        a, b, c, d = w4[k, 0, 0], w4[k, 0, 1], w4[k, 1, 0], w4[k, 1, 1]
        if transpose:
            b, c = c, b
        out[tt, o + h1, o + h1] = a
        out[tt, o + h1, o + h2] = b
        out[tt, o + h2, o + h1] = c
        out[tt, o + h2, o + h2] = d
    return out


def _host_prep(inp):
    d = {}
    d["wqT"] = np.ascontiguousarray(np.asarray(inp["L0_attn_Wq"], np.float32).T)
    d["wkT"] = np.ascontiguousarray(np.asarray(inp["L0_attn_Wk"], np.float32).T)
    d["wvT"] = np.ascontiguousarray(np.asarray(inp["L0_attn_Wv"], np.float32).T)
    lg = np.asarray(inp["L0_attn_ln_gamma"], np.float32)
    lb = np.asarray(inp["L0_attn_ln_beta"], np.float32)
    d["gbc"] = np.ascontiguousarray(np.stack([
        np.broadcast_to(lg[None, :], (128, E)),
        np.broadcast_to(lb[None, :], (128, E))]))
    bt = np.asarray(inp["L0_attn_bias_table"], np.float32)
    m = np.arange(4096)
    teff = bt[np.minimum(m, 2046), :]
    p = np.arange(128)
    idx = m[None, :] - p[:, None]
    rall = teff[np.clip(idx, 0, 4095), :]
    rall = np.where((idx >= 0)[:, :, None], rall, 0.0)
    d["rall"] = np.ascontiguousarray(rall.transpose(2, 0, 1))
    wexp = np.asarray(inp["L0_ffn_W_exp"], np.float32)
    aug = np.zeros((DHID, 1040), np.float32)
    aug[:, 0:1024] = wexp.T
    aug[:, 1024] = np.asarray(inp["L0_ffn_b_exp"], np.float32)
    d["wexpT_aug"] = aug
    nidx = np.arange(DHID)
    oon = 128 * (nidx % 32) + nidx // 32
    d["wcon_lo"] = np.ascontiguousarray(
        np.asarray(inp["L0_ffn_W_con"], np.float32)[oon])
    d["upbias_lo"] = np.ascontiguousarray(
        np.asarray(inp["L0_ffn_up_bias"], np.float32)[oon].reshape(32, 128).T)
    d["bconf"] = np.ascontiguousarray(
        np.asarray(inp["L0_ffn_b_con"], np.float32).reshape(8, 128).T)
    d["dbiasf"] = np.ascontiguousarray(
        np.asarray(inp["L0_ffn_down_bias"], np.float32).reshape(8, 128).T)
    wu = np.asarray(inp["L0_ffn_up_weights"], np.float32)
    wd = np.asarray(inp["L0_ffn_down_weights"], np.float32)

    def _ff(st):
        n = st.shape[0]
        return np.ascontiguousarray(
            st.reshape(n, 4, 8, 128, 128).transpose(0, 1, 3, 2, 4)
            .reshape(n, 4, 128, 1024))

    d["upl"] = _ff(np.stack([_low_blocks(wu, 6, True)] +
                            [_low_blocks(wu, k, False)
                             for k in range(5, -1, -1)]))
    d["uph"] = _ff(np.stack([_high_blocks(wu, 11, True)] +
                            [_high_blocks(wu, 7 + j, False)
                             for j in range(3, -1, -1)]))
    d["dnh"] = _ff(np.stack([_high_blocks(wd, 7, False)] +
                            [_high_blocks(wd, 7 + j, True)
                             for j in range(1, 5)]))
    d["dnl"] = _ff(np.stack([_low_blocks(wd, 0, False)] +
                            [_low_blocks(wd, k, True) for k in range(1, 7)]))
    d["eye"] = np.eye(128, dtype=np.float32)
    d["ones"] = np.ones((128, 128), np.float32)
    return d


def _ap(h, off, dims):
    return bass.AP(h, off, [list(x) for x in dims])


def build_program(taps=(), race=False,
                  phases=("init", "fold", "pre", "ln1", "kproj", "vproj", "qproj", "att", "ffn")):
    import os
    _ph = set(os.environ.get("KPHASES", ",".join(phases)).split(","))
    nc = bass.Bass("TRN2", target_bir_lowering=False,
                   detect_race_conditions=race)
    ir = IR()

    din = {}
    for name, shape in (
        ("x", [BL, S, E]), ("wqT", [E, E]), ("wkT", [E, E]), ("wvT", [E, E]),
        ("gbc", [2, 128, E]), ("rall", [H, 128, 4096]),
        ("wexpT_aug", [DHID, 1040]), ("wcon_lo", [DHID, E]),
        ("upbias_lo", [128, 32]), ("bconf", [128, 8]), ("dbiasf", [128, 8]),
        ("upl", [7, 4, 128, 1024]), ("uph", [5, 4, 128, 1024]),
        ("dnh", [5, 4, 128, 1024]), ("dnl", [7, 4, 128, 1024]),
        ("eye", [128, 128]), ("ones", [128, 128]),
    ):
        din[name] = nc.dram_tensor(name, shape, F32, kind="ExternalInput")
    out_h = nc.dram_tensor("out", [BL, S, E], F32, kind="ExternalOutput")

    wq16 = nc.dram_tensor("wq16", [NP, 128, 1024], BF16, kind="Internal")
    wk16 = nc.dram_tensor("wk16", [NP, 128, 1024], BF16, kind="Internal")
    wv16 = nc.dram_tensor("wv16", [E, E], BF16, kind="Internal")
    r16 = nc.dram_tensor("r16", [H, 128, 4096], BF16, kind="Internal")
    zhb = nc.dram_tensor("zhb", [DHID, 1040], BF16, kind="Internal")
    v1hb = nc.dram_tensor("v1hb", [DHID, E], BF16, kind="Internal")
    weffh = nc.dram_tensor("weffh", [32, 128, 1024], BF16, kind="Internal")
    wconch = nc.dram_tensor("wconch", [8, 128, 4096], BF16, kind="Internal")
    q16 = nc.dram_tensor("q16", [NP, 128, S], BF16, kind="Internal")

    tap_h = {}
    for name, shape, tdt in taps:
        tap_h[name] = nc.dram_tensor("tap_" + name, list(shape), tdt,
                                     kind="ExternalOutput")

    ctx = ExitStack()
    with ctx:
        sb = lambda nm, sh, dt: ctx.enter_context(nc.sbuf_tensor(nm, sh, dt))
        pt = lambda nm, sh, dt=F32: ctx.enter_context(
            nc.psum_tensor(nm, sh, dt))

        xtm = sb("xtm", [128, NTT * E], F32)
        qarr = sb("qarr", [128, 16384], BF16)
        karr = sb("karr", [128, 16384], BF16)
        varr = sb("varr", [128, 16640], BF16)
        aotm = sb("aotm", [128, 4096], F32)
        trn = sb("trn", [128, 9984], BF16)
        trnf = trn.bitcast(F32)
        kf = karr.bitcast(F32)
        f32a = sb("f32a", [128, 2048], F32)
        stat = sb("stat", [128, 160], F32)
        cst = sb("cst", [128, 256], F32)
        cstb = sb("cstb", [128, 132], BF16)
        bvec = sb("bvec", [128, 96], F32)

        psA = [pt("psA0", [128, 512]), pt("psA1", [128, 512])]
        psB = [pt("psB0", [128, 512]), pt("psB1", [128, 512])]
        psO = pt("psO", [128, 512])
        psS = pt("psS", [128, 512])
        psC = pt("psC", [128, 512])
        psU = pt("psU", [128, 512], BF16)
        PSN = {id(psA[0]): "psA0", id(psA[1]): "psA1", id(psB[0]): "psB0",
               id(psB[1]): "psB1", id(psO): "psO", id(psS): "psS",
               id(psC): "psC", id(psU): "psU"}

        op = ir.op

        def dma(dst, src, r=(), w=(), tag="dma"):
            return op("q", lambda q, d=dst, s=src: q.dma_start(d, s),
                      r=r, w=w, tag=tag)

        def gdma(dst, src, r=(), w=(), tag="gdma"):
            return op("g", lambda g, d=dst, s=src: g.dma_start(d, s),
                      r=r, w=w, tag=tag)

        def tap(name, src_ap, rregs):
            if name in tap_h:
                t = tap_h[name]
                dst = t[:, :] if len(t.shape) == 2 else t[:, :, :]
                dma(dst, src_ap, r=rregs, w=("tap_" + name,), tag="tap")

        ident = cstb[:, 0:128]
        eyef = cst[:, 0:128]
        ones_col = cstb[:, 128:129]
        onesf_row = cst[0:1, 128:256]

        dma(cst[:, 0:128], din["eye"][:, :], w=rg("cst", 0, 128))
        dma(cst[:, 128:256], din["ones"][:, :], w=rg("cst", 128, 256))
        op("v", lambda v: v.tensor_copy(cstb[:, 0:128], cst[:, 0:128]),
           r=rg("cst", 0, 128), w=rg("cstb", 0, 128))
        op("v", lambda v: v.tensor_copy(cstb[:, 128:132], cst[:, 128:132]),
           r=rg("cst", 128, 132), w=rg("cstb", 128, 132))
        dma(bvec[:, 32:64], din["upbias_lo"][:, :], w=rg("bvec", 32, 64))
        dma(bvec[:, 64:72], din["bconf"][:, :], w=rg("bvec", 64, 72))
        dma(bvec[:, 72:80], din["dbiasf"][:, :], w=rg("bvec", 72, 80))
        op("v", lambda v: v.tensor_add(bvec[:, 80:88], bvec[:, 64:72],
                                       bvec[:, 72:80]),
           r=rg("bvec", 64, 80), w=rg("bvec", 80, 88))
        op("v", lambda v: v.memset(bvec[:, 88:89], LN_EPS),
           w=rg("bvec", 88, 89))

        cvt = [0]
        for wname, srch, dsth in (("wq16", din["wqT"], wq16),
                                  ("wk16", din["wkT"], wk16)):
            for ec in range(8):
                fs = 1024 * (cvt[0] % 2)
                cvt[0] += 1
                gdma(trn[:, fs:fs + 1024], srch[128 * ec:128 * (ec + 1), :],
                     w=rg("trn", fs, fs + 1024))
                dma(_ap(dsth, 128 * ec,
                        [(1024, 128), (131072, NP), (1, 128)]),
                    trn[:, fs:fs + 1024], r=rg("trn", fs, fs + 1024),
                    w=(wname,))
        for ec in range(8):
            fs = 1024 * (cvt[0] % 2)
            cvt[0] += 1
            gdma(trn[:, fs:fs + 1024], din["wvT"][128 * ec:128 * (ec + 1), :],
                 w=rg("trn", fs, fs + 1024))
            dma(wv16[128 * ec:128 * (ec + 1), :], trn[:, fs:fs + 1024],
                r=rg("trn", fs, fs + 1024), w=("wv16",))
        for h in range(H):
            for half in range(2):
                ts = 2048 + 2048 * (cvt[0] % 2)
                cvt[0] += 1
                gdma(trn[:, ts:ts + 2048],
                     din["rall"][h, :, 2048 * half:2048 * (half + 1)],
                     w=rg("trn", ts, ts + 2048))
                dma(r16[h, :, 2048 * half:2048 * (half + 1)],
                    trn[:, ts:ts + 2048],
                    r=rg("trn", ts, ts + 2048), w=("r16",))

        if "pre" in _ph:
            for i in range(NTT):
                dma(xtm[:, E * i:E * (i + 1)],
                    din["x"][0, 128 * i:128 * (i + 1), :],
                    r=("xin",), w=rg("xtm", E * i, E * (i + 1)), tag="xld")

        def fold_load_set(hsrc, hkey, i, sl):
            for g in range(4):
                src = _ap(hsrc, (i * 4 + g) * 128 * 1024,
                          [(1024, 128), (1, 1024)])
                gdma(qarr[:, 4096 * sl + 1024 * g:4096 * sl + 1024 * (g + 1)],
                     src, r=(hkey,),
                     w=rg("qarr", 4096 * sl + 1024 * g,
                          4096 * sl + 1024 * (g + 1)), tag="fold_ld")

        def compose_piece(hsrc, hkey, base, i):
            if i == 0:
                fold_load_set(hsrc, hkey, 0, 0)
                op("v", lambda v, base=base: v.tensor_copy(
                    varr[:, base:base + 4096], qarr[:, 0:4096]),
                   r=rg("qarr", 0, 4096), w=rg("varr", base, base + 4096))
                return
            sl = i % 4
            fold_load_set(hsrc, hkey, i, sl)
            for blk in range(32):
                pn = PSN[id(psA[blk % 2])]
                op("t", lambda t, base=base, blk=blk, sl=sl: t.matmul(
                    psA[blk % 2][:, 0:128],
                    qarr[:, 4096 * sl + 128 * blk:
                         4096 * sl + 128 * (blk + 1)],
                    varr[:, base + 128 * blk:base + 128 * (blk + 1)],
                    start=True, stop=True),
                    r=rg("qarr", 4096 * sl + 128 * blk,
                         4096 * sl + 128 * (blk + 1))
                    + rg("varr", base + 128 * blk, base + 128 * (blk + 1)),
                    w=pg(pn), tag="fmm")
                op("v", lambda v, base=base, blk=blk: v.tensor_copy(
                    varr[:, base + 128 * blk:base + 128 * (blk + 1)],
                    psA[blk % 2][:, 0:128]),
                    r=pg(pn),
                    w=rg("varr", base + 128 * blk, base + 128 * (blk + 1)),
                    tag="fev")

        MMCH = ((0, 512, 0), (512, 512, 1), (1024, 16, 2))
        PSETS = ((psA[0], psA[1], psB[0]), (psB[1], psO, psS))
        SRC0 = 4096

        def fold_apply(base, srcfn, srckey, nch, outfn, u):
            sl = u % 2
            so = SRC0 + 1152 * (u % 4)
            srcfn(u, so)
            for (o, wd, pi) in MMCH[:nch]:
                pb = PSETS[sl][pi]
                pn = PSN[id(pb)]
                op("t", lambda t, u=u, o=o, wd=wd, pb=pb, so=so: t.matmul(
                    pb[:, 0:wd], varr[:, base + 128 * u:base + 128 * (u + 1)],
                    trn[:, so + o:so + o + wd], start=True, stop=True),
                    r=rg("varr", base + 128 * u, base + 128 * (u + 1))
                    + rg("trn", so + o, so + o + wd),
                    w=pg(pn), tag="fmm")
            outfn(u)

        def ua1_src(blk, so):
            gdma(trn[:, so:so + 1040],
                 din["wexpT_aug"][128 * blk:128 * (blk + 1), :],
                 r=("wexpT",), w=rg("trn", so, so + 1040), tag="fold")

        def ua1_out(blk):
            oo = 2048 * (blk % 2)
            ps = PSETS[blk % 2]
            op("v", lambda v, oo=oo, ps=ps: v.tensor_copy(
                trn[:, oo:oo + 512], ps[0][:, 0:512]),
               r=pg(PSN[id(ps[0])]), w=rg("trn", oo, oo + 512))
            op("v", lambda v, oo=oo, ps=ps: v.tensor_copy(
                trn[:, oo + 512:oo + 1024], ps[1][:, 0:512]),
               r=pg(PSN[id(ps[1])]), w=rg("trn", oo + 512, oo + 1024))
            op("v", lambda v, oo=oo, ps=ps: v.tensor_copy(
                trn[:, oo + 1024:oo + 1040], ps[2][:, 0:16]),
               r=pg(PSN[id(ps[2])]), w=rg("trn", oo + 1024, oo + 1040))
            dma(_ap(zhb, 1040 * blk, [(32 * 1040, 128), (1, 1040)]),
                trn[:, oo:oo + 1040], r=rg("trn", oo, oo + 1040), w=("zhb",),
                tag="fold_sc")


        def ua2_src(t_, so):
            dma(trn[:, so:so + 1040], zhb[128 * t_:128 * (t_ + 1), :],
                r=("zhb",), w=rg("trn", so, so + 1040), tag="fold")

        def ua2_out(t_):
            oo = 1024 * (t_ % 2)
            to = 2048 + 1024 * (t_ % 2)
            ps = PSETS[t_ % 2]
            op("v", lambda v, oo=oo, ps=ps: v.tensor_copy(
                trn[:, oo:oo + 512], ps[0][:, 0:512]),
               r=pg(PSN[id(ps[0])]), w=rg("trn", oo, oo + 512))
            op("v", lambda v, oo=oo, ps=ps: v.tensor_copy(
                trn[:, oo + 512:oo + 1024], ps[1][:, 0:512]),
               r=pg(PSN[id(ps[1])]), w=rg("trn", oo + 512, oo + 1024))
            op("v", lambda v, t_=t_, ps=ps: v.tensor_copy(
                bvec[:, t_:t_ + 1], ps[2][:, 0:1]),
               r=pg(PSN[id(ps[2])]), w=rg("bvec", t_, t_ + 1))
            for g2 in range(2):
                for q4 in range(4):
                    ec = 4 * g2 + q4
                    op("t", lambda t, oo=oo, ec=ec, q4=q4: t.transpose(
                        psU[:, 128 * q4:128 * (q4 + 1)],
                        trn[:, oo + 128 * ec:oo + 128 * (ec + 1)], ident),
                        r=rg("trn", oo + 128 * ec, oo + 128 * (ec + 1))
                        + rg("cstb", 0, 128), w=("psUbank",), tag="ftr")
                op("v", lambda v, to=to, g2=g2: v.tensor_copy(
                    trn[:, to + 512 * g2:to + 512 * (g2 + 1)],
                    psU[:, 0:512]),
                    r=("psUbank",),
                    w=rg("trn", to + 512 * g2, to + 512 * (g2 + 1)))
            dma(weffh[t_, :, :], trn[:, to:to + 1024],
                r=rg("trn", to, to + 1024), w=("weffh",), tag="fold_st")


        def da1_src(t_, so):
            gdma(trn[:, so:so + 1024],
                 din["wcon_lo"][128 * t_:128 * (t_ + 1), :],
                 r=("wconlo",), w=rg("trn", so, so + 1024), tag="fold")

        def da1_out(t_):
            oo = 1024 * (t_ % 4)
            ps = PSETS[t_ % 2]
            op("v", lambda v, oo=oo, ps=ps: v.tensor_copy(
                trn[:, oo:oo + 512], ps[0][:, 0:512]),
               r=pg(PSN[id(ps[0])]), w=rg("trn", oo, oo + 512))
            op("v", lambda v, oo=oo, ps=ps: v.tensor_copy(
                trn[:, oo + 512:oo + 1024], ps[1][:, 0:512]),
               r=pg(PSN[id(ps[1])]), w=rg("trn", oo + 512, oo + 1024))
            dma(v1hb[128 * t_:128 * (t_ + 1), :], trn[:, oo:oo + 1024],
                r=rg("trn", oo, oo + 1024), w=("v1hb",), tag="fold_st")


        def da2_src(u, so):
            dma(trn[:, so:so + 1024],
                _ap(v1hb, 1024 * u, [(32 * 1024, 128), (1, 1024)]),
                r=("v1hb",), w=rg("trn", so, so + 1024), tag="fold")

        def da2_out(u):
            oo = 1024 * (u % 4)
            ps = PSETS[u % 2]
            op("v", lambda v, oo=oo, ps=ps: v.tensor_copy(
                trn[:, oo:oo + 512], ps[0][:, 0:512]),
               r=pg(PSN[id(ps[0])]), w=rg("trn", oo, oo + 512))
            op("v", lambda v, oo=oo, ps=ps: v.tensor_copy(
                trn[:, oo + 512:oo + 1024], ps[1][:, 0:512]),
               r=pg(PSN[id(ps[1])]), w=rg("trn", oo + 512, oo + 1024))
            for et in range(8):
                dst = _ap(wconch, et * 32 * 128 * 128 + u * 4096,
                          [(128, 32), (131072, 4), (1, 128)])
                dma(dst, trn[:, oo + 128 * et:oo + 128 * (et + 1)],
                    r=rg("trn", oo + 128 * et, oo + 128 * (et + 1)),
                    w=("wconch",), tag="fold_sc")

        if "fold" in _ph:
            for i in range(7):
                compose_piece(din["upl"], "upl", 0, i)
            pend = [(din["uph"], "uph", 4096, i) for i in range(5)]
            for blk in range(32):
                fold_apply(0, ua1_src, "wexpT", 3, ua1_out, blk)
                if blk % 6 == 3 and pend:
                    compose_piece(*pend.pop(0))
            while pend:
                compose_piece(*pend.pop(0))
            pend = [(din["dnh"], "dnh", 8192, i) for i in range(5)]
            for t_ in range(32):
                fold_apply(4096, ua2_src, "zhb", 3, ua2_out, t_)
                if t_ % 6 == 3 and pend:
                    compose_piece(*pend.pop(0))
            while pend:
                compose_piece(*pend.pop(0))
            op("v", lambda v: v.tensor_add(bvec[:, 0:32], bvec[:, 0:32],
                                           bvec[:, 32:64]),
               r=rg("bvec", 0, 64), w=rg("bvec", 0, 32))
            pend = [(din["dnl"], "dnl", 12288, i) for i in range(7)]
            for t_ in range(32):
                fold_apply(8192, da1_src, "wconlo", 2, da1_out, t_)
                if t_ % 4 == 2 and pend:
                    compose_piece(*pend.pop(0))
            while pend:
                compose_piece(*pend.pop(0))
            for u in range(32):
                fold_apply(12288, da2_src, "v1hb", 2, da2_out, u)

        def ln_stats(c0, c1, nt):
            op("v", lambda v: v.tensor_scalar(
                stat[:, 96:96 + nt], stat[:, c0 + 16:c0 + 16 + nt],
                1.0 / E, None, ALU.mult),
                r=rg("stat", c0 + 16, c0 + 16 + nt), w=rg("stat", 96, 96 + nt))
            op("v", lambda v: v.tensor_scalar(
                stat[:, 112:112 + nt], stat[:, c0:c0 + nt],
                1.0 / E, None, ALU.mult),
                r=rg("stat", c0, c0 + nt), w=rg("stat", 112, 112 + nt))
            op("v", lambda v: v.tensor_mul(
                stat[:, 128:128 + nt], stat[:, 96:96 + nt],
                stat[:, 96:96 + nt]),
                r=rg("stat", 96, 96 + nt), w=rg("stat", 128, 128 + nt))
            op("v", lambda v: v.tensor_sub(
                stat[:, 112:112 + nt], stat[:, 112:112 + nt],
                stat[:, 128:128 + nt]),
                r=rg("stat", 112, 112 + nt) + rg("stat", 128, 128 + nt),
                w=rg("stat", 112, 112 + nt))
            op("s", lambda s: s.activation(
                stat[:, 128:128 + nt], stat[:, 112:112 + nt], AF.Sqrt,
                bias=bvec[:, 88:89], scale=1.0),
                r=rg("stat", 112, 112 + nt) + rg("bvec", 88, 89),
                w=rg("stat", 128, 128 + nt))
            op("v", lambda v: v.reciprocal(
                stat[:, c1:c1 + nt], stat[:, 128:128 + nt]),
                r=rg("stat", 128, 128 + nt), w=rg("stat", c1, c1 + nt))
            op("v", lambda v: v.tensor_mul(
                stat[:, 144:144 + nt], stat[:, 96:96 + nt],
                stat[:, c1:c1 + nt]),
                r=rg("stat", 96, 96 + nt) + rg("stat", c1, c1 + nt),
                w=rg("stat", 144, 144 + nt))
            op("v", lambda v: v.tensor_scalar(
                stat[:, c1 + 16:c1 + 16 + nt], stat[:, 144:144 + nt],
                -1.0, None, ALU.mult),
                r=rg("stat", 144, 144 + nt),
                w=rg("stat", c1 + 16, c1 + 16 + nt))

        SQ = 8960
        for b in (range(BL) if "pre" in _ph else ()):
            for i in range(NTT):
                if b > 0:
                    dma(xtm[:, E * i:E * (i + 1)],
                        din["x"][b, 128 * i:128 * (i + 1), :],
                        r=("xin",), w=rg("xtm", E * i, E * (i + 1)),
                        tag="xld")

            for i in range(NTT):
                op("s", lambda s, i=i: s.activation(
                    trn[:, SQ:SQ + 1024], xtm[:, E * i:E * (i + 1)],
                    AF.Square, accum_out=stat[:, i:i + 1]),
                    r=rg("xtm", E * i, E * (i + 1)),
                    w=rg("trn", SQ, SQ + 1024) + rg("stat", i, i + 1),
                    tag="ln1")
                op("v", lambda v, i=i: v.reduce_sum(
                    stat[:, 16 + i:17 + i], xtm[:, E * i:E * (i + 1)],
                    axis=AX.X),
                    r=rg("xtm", E * i, E * (i + 1)),
                    w=rg("stat", 16 + i, 17 + i), tag="ln1")
            if "ln1" in _ph:
                ln_stats(0, 48, NTT)
            for i in (range(NTT) if "ln1" in _ph else ()):
                ao = SQ if i % 2 == 0 else 4096
                op("v", lambda v, i=i, ao=ao: v.tensor_scalar(
                    trn[:, ao:ao + 1024], xtm[:, E * i:E * (i + 1)],
                    stat[:, 48 + i:49 + i], stat[:, 64 + i:65 + i],
                    ALU.mult, ALU.add),
                    r=rg("xtm", E * i, E * (i + 1)) + rg("stat", 48 + i, 49 + i)
                    + rg("stat", 64 + i, 65 + i),
                    w=rg("trn", ao, ao + 1024), tag="ln1a")
                for g2 in range(2):
                    qw = ()
                    for q4 in range(4):
                        ec = 4 * g2 + q4
                        qw += rg("qarr", 2048 * ec + 128 * i,
                                 2048 * ec + 128 * (i + 1))
                        op("t", lambda t, ec=ec, q4=q4, ao=ao: t.transpose(
                            psU[:, 128 * q4:128 * (q4 + 1)],
                            trn[:, ao + 128 * ec:ao + 128 * (ec + 1)], ident),
                            r=rg("trn", ao + 128 * ec, ao + 128 * (ec + 1))
                            + rg("cstb", 0, 128), w=("psUbank",), tag="xhT")
                    op("v", lambda v, i=i, g2=g2: v.tensor_copy(
                        _ap(qarr, 2048 * 4 * g2 + 128 * i,
                            [(16384, 128), (2048, 4), (1, 128)]),
                        psU[:, 0:512].rearrange("p (c f) -> p c f", c=4)),
                        r=("psUbank",), w=qw, tag="xhTe")
            if b == 0:
                tap("xhfm", qarr[:, 0:1024], rg("qarr", 0, 1024))

            for j in (range(NP) if "kproj" in _ph else ()):
                ko = 1024 * (j % 2)
                dma(trn[:, ko:ko + 1024],
                    _ap(wk16, j * 128 * 1024,
                        [(1024, 128), (1, 1024)]),
                    r=("wk16",), w=rg("trn", ko, ko + 1024), tag="wkld")
                for kc in range(4):
                    pn = PSN[id(psA[kc % 2])]
                    for ec in range(8):
                        op("t", lambda t, kc=kc, ec=ec, ko=ko: t.matmul(
                            psA[kc % 2][:, 0:512],
                            trn[:, ko + 128 * ec:ko + 128 * (ec + 1)],
                            qarr[:, 2048 * ec + 512 * kc:
                                 2048 * ec + 512 * (kc + 1)],
                            start=(ec == 0), stop=(ec == 7)),
                            r=rg("trn", ko + 128 * ec, ko + 128 * (ec + 1))
                            + rg("qarr", 2048 * ec + 512 * kc,
                                 2048 * ec + 512 * (kc + 1)),
                            w=pg(pn), accum=(ec != 0), tag="kmm")
                    op("v", lambda v, j=j, kc=kc: v.tensor_copy(
                        karr[:, 2048 * j + 512 * kc:2048 * j + 512 * (kc + 1)],
                        psA[kc % 2][:, 0:512]),
                        r=pg(pn),
                        w=rg("karr", 2048 * j + 512 * kc,
                             2048 * j + 512 * (kc + 1)), tag="kev")
            if b == 0:
                tap("kall", karr[:, 0:1024], rg("karr", 0, 1024))

            for ec in (range(8) if "vproj" in _ph else ()):
                dma(trn[:, 1024 * ec:1024 * (ec + 1)],
                    wv16[128 * ec:128 * (ec + 1), :],
                    r=("wv16",),
                    w=rg("trn", 1024 * ec, 1024 * (ec + 1)),
                    tag="wvld")
            if "vproj" in _ph:
                op("v", lambda v: v.memset(
                    _ap(varr, 64, [(16640, 128), (65, 256), (1, 1)]), 1.0),
                    w=rg("varr", 0, 16640), tag="vones")
            for kt in (range(NKT) if "vproj" in _ph else ()):
                for fc in range(2):
                    pn = PSN[id(psB[fc])]
                    for ec in range(8):
                        op("t", lambda t, kt=kt, fc=fc, ec=ec: t.matmul(
                            psB[fc][:, 0:512],
                            qarr[:, 2048 * ec + 128 * kt:
                                 2048 * ec + 128 * (kt + 1)],
                            trn[:, 1024 * ec + 512 * fc:
                                 1024 * ec + 512 * (fc + 1)],
                            start=(ec == 0), stop=(ec == 7)),
                            r=rg("qarr", 2048 * ec + 128 * kt,
                                 2048 * ec + 128 * (kt + 1))
                            + rg("trn", 1024 * ec + 512 * fc,
                                 1024 * ec + 512 * (fc + 1)),
                            w=pg(pn), accum=(ec != 0), tag="vmm")
                    op("v", lambda v, kt=kt, fc=fc: v.tensor_copy(
                        _ap(varr, 1040 * kt + 520 * fc,
                            [(16640, 128), (65, 8), (1, 64)]),
                        psB[fc][:, 0:512].rearrange("p (h d) -> p h d", h=8)),
                        r=pg(pn),
                        w=rg("varr", 1040 * kt + 520 * fc,
                             1040 * kt + 520 * (fc + 1)), tag="vev")
            if b == 0:
                tap("vtm", _ap(varr, 0, [(16640, 128), (65, 16), (1, 64)]),
                    rg("varr", 0, 1040))

            for j in (range(NP) if "qproj" in _ph else ()):
                ko = 3072 * (j % 2)
                dma(trn[:, ko:ko + 1024],
                    _ap(wq16, j * 128 * 1024,
                        [(1024, 128), (1, 1024)]),
                    r=("wq16",), w=rg("trn", ko, ko + 1024), tag="wqld")
                qo = 1024 + 3584 * (j % 2)
                for kc in range(4):
                    pn = PSN[id(psA[kc % 2])]
                    for ec in range(8):
                        op("t", lambda t, kc=kc, ec=ec, ko=ko: t.matmul(
                            psA[kc % 2][:, 0:512],
                            trn[:, ko + 128 * ec:ko + 128 * (ec + 1)],
                            qarr[:, 2048 * ec + 512 * kc:
                                 2048 * ec + 512 * (kc + 1)],
                            start=(ec == 0), stop=(ec == 7)),
                            r=rg("trn", ko + 128 * ec, ko + 128 * (ec + 1))
                            + rg("qarr", 2048 * ec + 512 * kc,
                                 2048 * ec + 512 * (kc + 1)),
                            w=pg(pn), accum=(ec != 0), tag="qmm")
                    op("v", lambda v, kc=kc, qo=qo: v.tensor_copy(
                        trn[:, qo + 512 * kc:qo + 512 * (kc + 1)],
                        psA[kc % 2][:, 0:512]),
                        r=pg(pn),
                        w=rg("trn", qo + 512 * kc, qo + 512 * (kc + 1)),
                        tag="qev")
                dma(q16[j, :, :], trn[:, qo:qo + 2048],
                    r=rg("trn", qo, qo + 2048), w=(f"q16_{j}",), tag="qst")
            for qc in (range(NQC) if "att" in _ph else ()):
                qb = QC * qc
                qbase = 4096 * (qc % 2)
                for j in range(NP):
                    dma(qarr[:, qbase + 512 * j:qbase + 512 * (j + 1)],
                        q16[j, :, qb:qb + 512], r=(f"q16_{j}",),
                        w=rg("qarr", qbase + 512 * j, qbase + 512 * (j + 1)),
                        tag="qld")
                if b == 0 and qc == 0:
                    tap("qall", qarr[:, 0:1024], rg("qarr", 0, 1024))
                for j in range(NP):
                    rodd = j % 2
                    rbuf, rb0, rnm = ((qarr, 8192, "qarr") if rodd
                                      else (trn, 0, "trn"))
                    for hh in range(2):
                        dma(rbuf[:, rb0 + RW * hh:rb0 + RW * (hh + 1)],
                            r16[2 * j + hh, :, qb + 127:qb + 127 + RW],
                            r=("r16",),
                            w=rg(rnm, rb0 + RW * hh, rb0 + RW * (hh + 1)),
                            tag="rwin")
                    kreg = rg("karr", 2048 * j, 2048 * (j + 1))
                    qreg = rg("qarr", qbase + 512 * j, qbase + 512 * (j + 1))

                    def sc_ops(kt, j=j, qb=qb, qbase=qbase,
                               kreg=kreg, qreg=qreg):
                        par = kt % 2
                        pnA, pnB = PSN[id(psA[par])], PSN[id(psB[par])]
                        eoA = 4864 + 512 * (kt % 4)
                        eoB = 6912 + 512 * (kt % 4)
                        op("t", lambda t, j=j, kt=kt, par=par,
                           qbase=qbase: t.matmul(
                            psA[par][:, 0:512],
                            karr[0:64, 2048 * j + 128 * kt:
                                 2048 * j + 128 * (kt + 1)],
                            qarr[0:64, qbase + 512 * j:qbase + 512 * (j + 1)],
                            start=True, stop=True, tile_position=(0, 0)),
                            r=kreg + qreg, w=pg(pnA), tag="sc")
                        op("t", lambda t, j=j, kt=kt, par=par,
                           qbase=qbase: t.matmul(
                            psB[par][:, 0:512],
                            karr[64:128, 2048 * j + 128 * kt:
                                 2048 * j + 128 * (kt + 1)],
                            qarr[64:128, qbase + 512 * j:
                                 qbase + 512 * (j + 1)],
                            start=True, stop=True, tile_position=(64, 0)),
                            r=kreg + qreg, w=pg(pnB), tag="sc")
                        op("s", lambda s, eoA=eoA, par=par: s.activation(
                            trn[:, eoA:eoA + 512], psA[par][:, 0:512],
                            AF.Exp, scale=0.03125),
                            r=pg(pnA), w=rg("trn", eoA, eoA + 512), tag="exp")
                        op("s", lambda s, eoB=eoB, par=par: s.activation(
                            trn[:, eoB:eoB + 512], psB[par][:, 0:512],
                            AF.Exp, scale=0.03125),
                            r=pg(pnB), w=rg("trn", eoB, eoB + 512), tag="exp")

                    def avrb_ops(kt, j=j, rbuf=rbuf, rb0=rb0, rnm=rnm):
                        eoA = 4864 + 512 * (kt % 4)
                        eoB = 6912 + 512 * (kt % 4)
                        vb = 1040 * kt + 130 * j
                        vregA = rg("varr", vb, vb + 65)
                        vregB = rg("varr", vb + 65, vb + 130)
                        op("t", lambda t, kt=kt, vb=vb, eoA=eoA: t.matmul(
                            psO[0:65, :], varr[:, vb:vb + 65],
                            trn[:, eoA:eoA + 512],
                            start=(kt == 0), stop=(kt == NKT - 1),
                            tile_position=(0, 0)),
                            r=vregA + rg("trn", eoA, eoA + 512),
                            w=pg("psO", 0, 96), accum=(kt != 0), tag="av")
                        op("t", lambda t, kt=kt, vb=vb, eoB=eoB: t.matmul(
                            psS[0:65, :], varr[:, vb + 65:vb + 130],
                            trn[:, eoB:eoB + 512],
                            start=(kt == 0), stop=(kt == NKT - 1),
                            tile_position=(0, 0)),
                            r=vregB + rg("trn", eoB, eoB + 512),
                            w=pg("psS", 0, 96), accum=(kt != 0), tag="av")
                        roff = rb0 + 1920 - 128 * kt
                        op("t", lambda t, kt=kt, vb=vb, roff=roff,
                           rbuf=rbuf: t.matmul(
                            psC[0:64, :], varr[:, vb:vb + 64],
                            rbuf[:, roff:roff + 512],
                            start=(kt == 0), stop=(kt == NKT - 1),
                            tile_position=(0, 0)),
                            r=vregA + rg(rnm, roff, roff + 512),
                            w=pg("psC", 0, 64), accum=(kt != 0), tag="rb")
                        op("t", lambda t, kt=kt, vb=vb, roff=roff,
                           rbuf=rbuf: t.matmul(
                            psC[64:128, :], varr[:, vb + 65:vb + 129],
                            rbuf[:, RW + roff:RW + roff + 512],
                            start=(kt == 0), stop=(kt == NKT - 1),
                            tile_position=(0, 64), skip_group_check=True),
                            r=vregB + rg(rnm, RW + roff, RW + roff + 512),
                            w=pg("psC", 64, 128), accum=(kt != 0), tag="rb")

                    sc_ops(0)
                    sc_ops(1)
                    for kt in range(NKT):
                        avrb_ops(kt)
                        if kt + 2 < NKT:
                            sc_ops(kt + 2)

                    op("v", lambda v: v.reciprocal(f32a[64:65, 1024:1536],
                                                   psO[64:65, :]),
                       r=pg("psO", 64, 96), w=rg("f32a", 1024, 1536), tag="as")
                    op("v", lambda v: v.reciprocal(f32a[64:65, 1536:2048],
                                                   psS[64:65, :]),
                       r=pg("psS", 64, 96), w=rg("f32a", 1536, 2048), tag="as")
                    op("t", lambda t: t.matmul(
                        psA[0][0:64, :], cst[64:65, 128:192],
                        f32a[64:65, 1024:1536], start=True, stop=True,
                        tile_position=(64, 0)),
                        r=rg("f32a", 1024, 1536) + rg("cst", 128, 192),
                        w=pg("psA0", 0, 64), tag="as")
                    op("t", lambda t: t.matmul(
                        psA[0][64:128, :], cst[64:65, 128:192],
                        f32a[64:65, 1536:2048], start=True, stop=True,
                        tile_position=(64, 64), skip_group_check=True),
                        r=rg("f32a", 1536, 2048) + rg("cst", 128, 192),
                        w=pg("psA0", 64, 128), tag="as")
                    op("v", lambda v: v.tensor_copy(trnf[:, 4480:4992],
                                                    psA[0][:, 0:512]),
                       r=pg("psA0"), w=rg("trn", 8960, 9984), tag="as")
                    op("v", lambda v: v.tensor_tensor(
                        f32a[0:64, 0:512], psO[0:64, 0:512],
                        trnf[0:64, 4480:4992], ALU.mult),
                       r=pg("psO", 0, 64) + rg("trn", 8960, 9984),
                       w=rg("f32a", 0, 512), tag="as")
                    op("v", lambda v: v.tensor_tensor(
                        f32a[64:128, 0:512], psS[0:64, 0:512],
                        trnf[64:128, 4480:4992], ALU.mult),
                       r=pg("psS", 0, 64) + rg("trn", 8960, 9984),
                       w=rg("f32a", 0, 512), tag="as")
                    op("v", lambda v: v.tensor_tensor(
                        f32a[:, 512:1024], f32a[:, 0:512], psC[:, 0:512],
                        ALU.add),
                       r=rg("f32a", 0, 512) + pg("psC"),
                       w=rg("f32a", 512, 1024), tag="as")
                    for c in range(4):
                        op("t", lambda t, c=c: t.transpose(
                            psO[:, 128 * c:128 * (c + 1)],
                            f32a[:, 512 + 128 * c:512 + 128 * (c + 1)], eyef),
                            r=rg("f32a", 512 + 128 * c, 512 + 128 * (c + 1))
                            + rg("cst", 0, 128), w=pg("psO"), tag="asT")
                    aosegs = tuple(f"aotm:{(1024 * c + 128 * j) // 128}"
                                   for c in range(4))
                    op("v", lambda v, j=j: v.tensor_copy(
                        _ap(aotm, 128 * j, [(4096, 128), (1024, 4), (1, 128)]),
                        psO[:, 0:512].rearrange("p (c f) -> p c f", c=4)),
                        r=pg("psO"), w=aosegs, tag="asE")

                dma(f32a[:, 0:1024], din["gbc"][0, :, :], r=("gbc",),
                    w=rg("f32a", 0, 1024), tag="gbld")
                dma(trnf[:, 0:1024], din["gbc"][1, :, :], r=("gbc",),
                    w=rg("trn", 0, 2048), tag="gbld")
                for c in range(4):
                    aoreg = rg("aotm", 1024 * c, 1024 * (c + 1))
                    op("s", lambda s, c=c: s.activation(
                        trn[:, SQ:SQ + 1024], aotm[:, 1024 * c:1024 * (c + 1)],
                        AF.Square, accum_out=stat[:, c:c + 1]),
                        r=aoreg,
                        w=rg("trn", SQ, SQ + 1024) + rg("stat", c, c + 1),
                        tag="lnA")
                    op("v", lambda v, c=c: v.reduce_sum(
                        stat[:, 16 + c:17 + c],
                        aotm[:, 1024 * c:1024 * (c + 1)], axis=AX.X),
                        r=aoreg, w=rg("stat", 16 + c, 17 + c), tag="lnA")
                ln_stats(0, 48, 4)
                for c in range(4):
                    ti = 4 * qc + c
                    aoreg = rg("aotm", 1024 * c, 1024 * (c + 1))
                    op("v", lambda v, c=c: v.tensor_scalar(
                        f32a[:, 1024:2048], aotm[:, 1024 * c:1024 * (c + 1)],
                        stat[:, 48 + c:49 + c], stat[:, 64 + c:65 + c],
                        ALU.mult, ALU.add),
                        r=aoreg + rg("stat", 48 + c, 49 + c)
                        + rg("stat", 64 + c, 65 + c),
                        w=rg("f32a", 1024, 2048), tag="lnA2")
                    op("v", lambda v, c=c: v.tensor_mul(
                        aotm[:, 1024 * c:1024 * (c + 1)], f32a[:, 1024:2048],
                        f32a[:, 0:1024]),
                        r=rg("f32a", 1024, 2048) + rg("f32a", 0, 1024),
                        w=aoreg, tag="lnA2")
                    op("v", lambda v, c=c: v.tensor_add(
                        aotm[:, 1024 * c:1024 * (c + 1)],
                        aotm[:, 1024 * c:1024 * (c + 1)], trnf[:, 0:1024]),
                        r=aoreg + rg("trn", 0, 2048), w=aoreg, tag="lnA2")
                    op("s", lambda s, c=c: s.activation(
                        trn[:, SQ:SQ + 1024], aotm[:, 1024 * c:1024 * (c + 1)],
                        AF.Square, accum_out=stat[:, 32 + c:33 + c]),
                        r=aoreg,
                        w=rg("trn", SQ, SQ + 1024) + rg("stat", 32 + c, 33 + c),
                        tag="lnB")
                    op("v", lambda v, c=c: v.reduce_sum(
                        stat[:, 20 + c:21 + c],
                        aotm[:, 1024 * c:1024 * (c + 1)], axis=AX.X),
                        r=aoreg, w=rg("stat", 20 + c, 21 + c), tag="lnB")
                op("v", lambda v: v.tensor_scalar(
                    stat[:, 96:100], stat[:, 20:24], 1.0 / E, None, ALU.mult),
                    r=rg("stat", 20, 24), w=rg("stat", 96, 100))
                op("v", lambda v: v.tensor_scalar(
                    stat[:, 112:116], stat[:, 32:36], 1.0 / E, None, ALU.mult),
                    r=rg("stat", 32, 36), w=rg("stat", 112, 116))
                op("v", lambda v: v.tensor_mul(
                    stat[:, 128:132], stat[:, 96:100], stat[:, 96:100]),
                    r=rg("stat", 96, 100), w=rg("stat", 128, 132))
                op("v", lambda v: v.tensor_sub(
                    stat[:, 112:116], stat[:, 112:116], stat[:, 128:132]),
                    r=rg("stat", 112, 116) + rg("stat", 128, 132),
                    w=rg("stat", 112, 116))
                op("s", lambda s: s.activation(
                    stat[:, 128:132], stat[:, 112:116], AF.Sqrt,
                    bias=bvec[:, 88:89], scale=1.0),
                    r=rg("stat", 112, 116) + rg("bvec", 88, 89),
                    w=rg("stat", 128, 132))
                op("v", lambda v: v.reciprocal(stat[:, 80:84],
                                               stat[:, 128:132]),
                   r=rg("stat", 128, 132), w=rg("stat", 80, 84))
                op("v", lambda v: v.tensor_mul(
                    stat[:, 144:148], stat[:, 96:100], stat[:, 80:84]),
                    r=rg("stat", 96, 100) + rg("stat", 80, 84),
                    w=rg("stat", 144, 148))
                op("v", lambda v: v.tensor_scalar(
                    stat[:, 84:88], stat[:, 144:148], -1.0, None, ALU.mult),
                    r=rg("stat", 144, 148), w=rg("stat", 84, 88))
                for c in range(4):
                    ti = 4 * qc + c
                    aoreg = rg("aotm", 1024 * c, 1024 * (c + 1))
                    xreg = rg("xtm", E * ti, E * (ti + 1))
                    op("v", lambda v, c=c: v.tensor_scalar(
                        f32a[:, 1024:2048], aotm[:, 1024 * c:1024 * (c + 1)],
                        stat[:, 80 + c:81 + c], stat[:, 84 + c:85 + c],
                        ALU.mult, ALU.add),
                        r=aoreg + rg("stat", 80 + c, 81 + c)
                        + rg("stat", 84 + c, 85 + c),
                        w=rg("f32a", 1024, 2048), tag="lnB2")
                    op("v", lambda v, ti=ti: v.tensor_add(
                        xtm[:, E * ti:E * (ti + 1)],
                        xtm[:, E * ti:E * (ti + 1)], f32a[:, 1024:2048]),
                        r=xreg + rg("f32a", 1024, 2048), w=xreg, tag="resA")
                if b == 0 and qc == 0:
                    tap("x1", xtm[:, 0:1024], rg("xtm", 0, 1024))

            for tc in (range(NQC) if "ffn" in _ph else ()):
                for c in range(4):
                    ti = 4 * tc + c
                    xreg = rg("xtm", E * ti, E * (ti + 1))
                    op("s", lambda s, ti=ti, c=c: s.activation(
                        trn[:, SQ:SQ + 1024], xtm[:, E * ti:E * (ti + 1)],
                        AF.Square, accum_out=stat[:, c:c + 1]),
                        r=xreg,
                        w=rg("trn", SQ, SQ + 1024) + rg("stat", c, c + 1),
                        tag="ln3")
                    op("v", lambda v, ti=ti, c=c: v.reduce_sum(
                        stat[:, 16 + c:17 + c], xtm[:, E * ti:E * (ti + 1)],
                        axis=AX.X),
                        r=xreg, w=rg("stat", 16 + c, 17 + c), tag="ln3")
                ln_stats(0, 48, 4)
                for c in range(4):
                    ti = 4 * tc + c
                    ao = 4096 if c % 2 == 0 else SQ
                    xreg = rg("xtm", E * ti, E * (ti + 1))
                    op("v", lambda v, ti=ti, c=c, ao=ao: v.tensor_scalar(
                        trn[:, ao:ao + 1024], xtm[:, E * ti:E * (ti + 1)],
                        stat[:, 48 + c:49 + c], stat[:, 64 + c:65 + c],
                        ALU.mult, ALU.add),
                        r=xreg + rg("stat", 48 + c, 49 + c)
                        + rg("stat", 64 + c, 65 + c),
                        w=rg("trn", ao, ao + 1024), tag="ln3a")
                    for g2 in range(2):
                        qw = ()
                        for q4 in range(4):
                            ec = 4 * g2 + q4
                            qw += rg("trn", 512 * ec + 128 * c,
                                     512 * ec + 128 * (c + 1))
                            op("t", lambda t, ec=ec, q4=q4, ao=ao: t.transpose(
                                psU[:, 128 * q4:128 * (q4 + 1)],
                                trn[:, ao + 128 * ec:ao + 128 * (ec + 1)],
                                ident),
                                r=rg("trn", ao + 128 * ec,
                                     ao + 128 * (ec + 1))
                                + rg("cstb", 0, 128), w=("psUbank",),
                                tag="x3T")
                        op("v", lambda v, c=c, g2=g2: v.tensor_copy(
                            _ap(trn, 512 * 4 * g2 + 128 * c,
                                [(9984, 128), (512, 4), (1, 128)]),
                            psU[:, 0:512].rearrange("p (q f) -> p q f", q=4)),
                            r=("psUbank",), w=qw, tag="x3Te")

                for ht in range(32):
                    wo = 5120 + 1024 * (ht % 2)
                    dma(trn[:, wo:wo + 1024],
                        _ap(weffh, ht * 128 * 1024,
                            [(1024, 128), (1, 1024)]),
                        r=("weffh",), w=rg("trn", wo, wo + 1024), tag="wefld")
                    pn = PSN[id(psA[ht % 2])]
                    for ec in range(8):
                        op("t", lambda t, ht=ht, ec=ec, wo=wo: t.matmul(
                            psA[ht % 2][:, 0:512],
                            trn[:, wo + 128 * ec:wo + 128 * (ec + 1)],
                            trn[:, 512 * ec:512 * (ec + 1)],
                            start=(ec == 0), stop=(ec == 7)),
                            r=rg("trn", wo + 128 * ec, wo + 128 * (ec + 1))
                            + rg("trn", 512 * ec, 512 * (ec + 1)),
                            w=pg(pn), accum=(ec != 0), tag="hmm")
                    op("s", lambda s, ht=ht: s.activation(
                        karr[:, 512 * ht:512 * (ht + 1)],
                        psA[ht % 2][:, 0:512], AF.Gelu,
                        bias=bvec[:, ht:ht + 1], scale=1.0),
                        r=pg(pn) + rg("bvec", ht, ht + 1),
                        w=rg("karr", 512 * ht, 512 * (ht + 1)), tag="gelu")
                if b == 0 and tc == 0:
                    tap("hid", karr[:, 0:1024], rg("karr", 0, 1024))

                for et in range(8):
                    wo = 4096 * (et % 2)
                    dma(trn[:, wo:wo + 4096],
                        _ap(wconch, et * 32 * 128 * 128,
                            [(4096, 128), (1, 4096)]),
                        r=("wconch",), w=rg("trn", wo, wo + 4096), tag="wccld")
                    pn = PSN[id(psB[et % 2])]
                    for ht in range(32):
                        op("t", lambda t, et=et, ht=ht, wo=wo: t.matmul(
                            psB[et % 2][:, 0:512],
                            trn[:, wo + 128 * ht:wo + 128 * (ht + 1)],
                            karr[:, 512 * ht:512 * (ht + 1)],
                            start=(ht == 0), stop=(ht == 31)),
                            r=rg("trn", wo + 128 * ht, wo + 128 * (ht + 1))
                            + rg("karr", 512 * ht, 512 * (ht + 1)),
                            w=pg(pn), accum=(ht != 0), tag="ymm")
                    op("v", lambda v, et=et: v.tensor_scalar(
                        aotm[:, 512 * et:512 * (et + 1)],
                        psB[et % 2][:, 0:512], bvec[:, 80 + et:81 + et],
                        None, ALU.add),
                        r=pg(pn) + rg("bvec", 80 + et, 81 + et),
                        w=rg("aotm", 512 * et, 512 * (et + 1)), tag="yev")
                if b == 0 and tc == 0:
                    tap("yfm", aotm[:, 0:1024], rg("aotm", 0, 1024))

                for c in range(4):
                    for g2 in range(2):
                        for q4 in range(4):
                            et = 4 * g2 + q4
                            op("t", lambda t, et=et, c=c, q4=q4: t.transpose(
                                psO[:, 128 * q4:128 * (q4 + 1)],
                                aotm[:, 512 * et + 128 * c:
                                     512 * et + 128 * (c + 1)], eyef),
                                r=rg("aotm", 512 * et + 128 * c,
                                     512 * et + 128 * (c + 1))
                                + rg("cst", 0, 128),
                                w=pg("psO"), tag="yT")
                        ko = 8192 + 2048 * c + 1024 * g2
                        op("v", lambda v, c=c, g2=g2: v.tensor_copy(
                            kf[:, 4096 + 1024 * c + 512 * g2:
                               4096 + 1024 * c + 512 * (g2 + 1)],
                            psO[:, 0:512]),
                            r=pg("psO"),
                            w=rg("karr", ko, ko + 1024), tag="yTe")
                for c in range(4):
                    kr = rg("karr", 8192 + 2048 * c, 8192 + 2048 * (c + 1))
                    op("s", lambda s, c=c: s.activation(
                        trn[:, SQ:SQ + 1024],
                        kf[:, 4096 + 1024 * c:4096 + 1024 * (c + 1)],
                        AF.Square, accum_out=stat[:, c:c + 1]),
                        r=kr,
                        w=rg("trn", SQ, SQ + 1024) + rg("stat", c, c + 1),
                        tag="ln4")
                    op("v", lambda v, c=c: v.reduce_sum(
                        stat[:, 16 + c:17 + c],
                        kf[:, 4096 + 1024 * c:4096 + 1024 * (c + 1)],
                        axis=AX.X),
                        r=kr, w=rg("stat", 16 + c, 17 + c), tag="ln4")
                ln_stats(0, 48, 4)
                for c in range(4):
                    ti = 4 * tc + c
                    kr = rg("karr", 8192 + 2048 * c, 8192 + 2048 * (c + 1))
                    op("v", lambda v, c=c: v.tensor_scalar(
                        f32a[:, 1024:2048],
                        kf[:, 4096 + 1024 * c:4096 + 1024 * (c + 1)],
                        stat[:, 48 + c:49 + c], stat[:, 64 + c:65 + c],
                        ALU.mult, ALU.add),
                        r=kr + rg("stat", 48 + c, 49 + c)
                        + rg("stat", 64 + c, 65 + c),
                        w=rg("f32a", 1024, 2048), tag="ln4a")
                    xreg = rg("xtm", E * ti, E * (ti + 1))
                    op("v", lambda v, ti=ti: v.tensor_add(
                        xtm[:, E * ti:E * (ti + 1)],
                        xtm[:, E * ti:E * (ti + 1)], f32a[:, 1024:2048]),
                        r=xreg + rg("f32a", 1024, 2048), w=xreg, tag="resF")
                    dma(out_h[b, 128 * ti:128 * (ti + 1), :],
                        xtm[:, E * ti:E * (ti + 1)], r=xreg,
                        w=(f"out{b}_{ti}",), tag="outst")

        if "ffn" not in _ph or "pre" not in _ph:
            for ti in range(2):
                dma(xtm[:, E * ti:E * (ti + 1)],
                    din["x"][0, 128 * ti:128 * (ti + 1), :],
                    r=("xin",), w=rg("xtm", E * ti, E * (ti + 1)))
                dma(out_h[0, 128 * ti:128 * (ti + 1), :],
                    xtm[:, E * ti:E * (ti + 1)],
                    r=rg("xtm", E * ti, E * (ti + 1)),
                    w=(f"outz{ti}",), tag="outst")
        ir.finalize(nc, ctx)
    return nc, tap_h, ir


_CACHED = {}


def kernel(**inputs):
    x = np.asarray(inputs["L0_attn_x"], np.float32)
    prep = _host_prep(inputs)
    if "nc" not in _CACHED:
        _CACHED["nc"] = build_program()[0]
    nc = _CACHED["nc"]
    in_maps = []
    for c in range(NCORES):
        m = {"x": np.ascontiguousarray(x[BL * c:BL * (c + 1)])}
        m.update(prep)
        in_maps.append(m)
    res = run_bass_kernel_spmd(nc, in_maps, core_ids=list(range(NCORES)))
    return np.concatenate([r["out"] for r in res.results], axis=0)
